# version 16
# baseline (speedup 1.0000x reference)
import math
import os
import numpy as np
import ml_dtypes

import concourse.bass as bass
import concourse.mybir as mybir
from concourse.bass_utils import run_bass_kernel_spmd
from concourse.alu_op_type import AluOpType

F32 = mybir.dt.float32
BF16 = mybir.dt.bfloat16
I32 = mybir.dt.int32
AF = mybir.ActivationFunctionType
AX = mybir.AxisListType
OP = AluOpType

B, S, E, H, DH = 4, 4096, 1024, 8, 128
NF = 64
TOK = 2048
TC = 512
NCH = TOK // TC
ET = E // 128
HID = 3072
HT = HID // 128
KC = S // 128
QT = S // 512
EPS = 1e-5
QDEQ = 1.0 / 127.0
ISQ = 1.0 / 32.0

ENGINES = ("sync", "tensor", "vector", "scalar", "gpsimd")


def _make_consts():
    div = np.exp(np.arange(0, E, 2, dtype=np.float64) * (-math.log(10000.0) / E))
    pca_lo = np.zeros((H * 128, S), np.float64)
    psa_lo = np.zeros((H * 128, S), np.float64)
    pca_up = np.zeros((H * 128, S), np.float64)
    psa_up = np.zeros((H * 128, S), np.float64)
    pe_hi = np.zeros((H * 128, S), np.float64)
    q_idx = np.arange(S, dtype=np.float64)
    for h in range(H):
        w = div[h * NF:(h + 1) * NF]
        a_lo = q_idx[None, :] * w[:, None]
        a_up = (q_idx[None, :] + S + 1) * w[:, None]
        a_hi = (q_idx[None, :] + S - 1) * w[:, None]
        for half in (0, 64):
            pca_lo[h * 128 + half:h * 128 + half + 64] = np.cos(a_lo)
            psa_lo[h * 128 + half:h * 128 + half + 64] = np.sin(a_lo)
            pca_up[h * 128 + half:h * 128 + half + 64] = np.cos(a_up)
            psa_up[h * 128 + half:h * 128 + half + 64] = np.sin(a_up)
        pe_hi[h * 128:h * 128 + 64] = np.sin(a_hi)
        pe_hi[h * 128 + 64:h * 128 + 128] = np.cos(a_hi)
    ki = np.arange(128)[:, None]
    jj = np.arange(128)[None, :]
    bf = ml_dtypes.bfloat16
    return dict(
        pca_lo=pca_lo.astype(bf), psa_lo=psa_lo.astype(bf),
        pca_up=pca_up.astype(bf), psa_up=psa_up.astype(bf), pe_hi=pe_hi.astype(bf),
        m_lo=(ki < jj).astype(np.float32), m_up=(ki > jj).astype(np.float32),
        ident=np.eye(128, dtype=np.float32).astype(bf),
        ones_col=np.ones((128, 1), np.float32).astype(bf),
        ones_row=np.ones((1, 128), np.float32).astype(bf),
    )


_CONSTS = None


def _consts():
    global _CONSTS
    if _CONSTS is None:
        _CONSTS = _make_consts()
    return _CONSTS


class W:

    def __init__(self, target, eng, sems):
        self.t = target
        self.e = eng
        self.sems = sems
        self.n = {k: 0 for k in sems}

    def wait(self, engine, sem, val=None):
        v = self.n[sem] if val is None else val
        if v <= 0:
            return
        if self.t == engine:
            self.e.wait_ge(self.sems[sem], v)

    def emit(self, engine, fn, inc=None, inc_n=1):
        if engine == "vector" or engine == "scalar":
            sem = "vS" if engine == "vector" else "cS"
            assert inc in (None, sem), (engine, inc)
            if self.t == engine:
                if self.n[sem] > 0:
                    self.e.wait_ge(self.sems[sem], self.n[sem])
                inst = fn(self.e)
                inst.then_inc(self.sems[sem], 1)
            self.n[sem] += 1
            return
        if self.t == engine:
            inst = fn(self.e)
            if inc is not None:
                inst.then_inc(self.sems[inc], inc_n)
        if inc is not None:
            self.n[inc] += inc_n

    def dma(self, engine, out, in_, sem, inc_n=16):
        self.emit(engine, lambda e: e.dma_start(out=out, in_=in_),
                  inc=sem, inc_n=inc_n)

    DATA_SEMS = ("tS", "vS", "cS", "sS", "gS", "sT", "sH", "sC", "sX2", "sXA",
                 "sXB", "sRA", "sRB", "sWA", "sWB", "sGA", "sGB", "cc", "sSW",
                 "sHA", "sHB", "sUA", "sUB",
                 "sO0", "sO1", "sO2", "sO3", "sO4", "sO5", "sO6", "sO7")

    def barrier(self):
        for eng in ENGINES:
            for sm in self.DATA_SEMS:
                self.wait(eng, sm)


PHASES = os.environ.get("MK_PHASES", "all")
P2LVL = int(os.environ.get("MK_P2", "99"))
HSLVL = int(os.environ.get("MK_HS", "99"))
NOSTRIP = os.environ.get("MK_NOSTRIP", "0") == "1"


def _program(w, T, D, PB):
    def P(i):
        return PB[i]

    def finish_early():
        w.wait("gpsimd", "vS", w.n["vS"])
        w.wait("gpsimd", "tS", w.n["tS"])
        w.wait("gpsimd", "cS", w.n["cS"])
        w.wait("gpsimd", "sS", w.n["sS"])
        w.wait("gpsimd", "gS", w.n["gS"])
        for et in range(ET):
            for tc in range(NCH):
                w.dma("gpsimd", D["out"][et * 128:(et + 1) * 128,
                                         tc * TC:(tc + 1) * TC], T["xf0"], "gS")
        for eng in ENGINES:
            w.wait(eng, "gS", w.n["gS"])

    for nm in ("ident", "m_lo", "m_up", "ones_col", "ones_row", "bq", "bk",
               "bv", "ub", "vb", "beta_col", "bo_col", "gamma_col", "selA",
               "selB", "immc", "selT"):
        w.dma("sync", T[nm], D[nm], "sS")
    for src in ("wq", "wk", "wv", "wab"):
        w.wait("sync", "vS", w.n["vS"])
        w.dma("sync", T["xf0"][:, 0:128], D[src], "sS")
        w.wait("vector", "sS", w.n["sS"])
        w.emit("vector", lambda e, src=src: e.tensor_copy(
            T[src], T["xf0"][:, 0:128]), inc="vS")
    w.wait("sync", "vS", w.n["vS"])
    w.dma("sync", T["xf01_row"], D["gamma"], "sS")
    w.wait("vector", "sS", w.n["sS"])
    w.emit("vector", lambda e: e.tensor_copy(T["gamma_row"], T["xf01_row"]), inc="vS")
    w.emit("vector", lambda e: e.tensor_scalar_add(T["quv"][:, 0:4], T["ub"], T["bq"]))
    w.emit("vector", lambda e: e.tensor_scalar_add(T["quv"][:, 4:8], T["vb"], T["bq"]),
           inc="vS")
    w.emit("vector", lambda e: e.tensor_scalar(
        T["selT2"], T["selT"], -1.0, 1.0, OP.mult, OP.add), inc="vS")
    w.wait("sync", "vS", w.n["vS"])

    stbf_free = [0, 0]
    for i in range(ET):
        w.wait("sync", "vS", w.n["vS"])
        w.dma("sync", T["xf01_blk"], D["wo"][i * 128:(i + 1) * 128, :], "sS")
        w.wait("vector", "gS", w.n["gS"])
        w.wait("vector", "sS", w.n["sS"])
        w.emit("vector", lambda e, i=i: e.tensor_copy(
            T["stagebf"][i % 2], T["xf01_blk"]), inc="vS")
        w.wait("gpsimd", "vS", w.n["vS"])
        w.dma("gpsimd", D["wo_bf"][i * 128:(i + 1) * 128, :], T["stagebf"][i % 2],
              "gS")
        stbf_free[i % 2] = w.n["gS"]
    xf_free = w.n["vS"]

    cast_after = []

    def stage_guard(k):
        if k >= 2:
            w.wait("sync", "vS", cast_after[k - 2])

    kstage = [0]

    def conv_big(dst_tiles, dram, ntiles, cols):
        hc = cols // 2
        for i in range(ntiles):
            for hf in range(2):
                k = kstage[0]
                buf = T["istage"][k % 2]
                sw = "sWA" if (k % 2 == 0) else "sWB"
                stage_guard(k)
                w.dma("sync", buf, dram[i * 128:(i + 1) * 128,
                                        hf * hc:(hf + 1) * hc], sw)
                w.wait("vector", sw, w.n[sw])
                w.emit("vector", lambda e, i=i, hf=hf, buf=buf, hc=hc: e.tensor_copy(
                    dst_tiles[i][:, hf * hc:(hf + 1) * hc], buf), inc="vS")
                cast_after.append(w.n["vS"])
                kstage[0] += 1

    conv_big(T["wg_t"], D["wg_i32"], ET, HID)
    w.wait("gpsimd", "vS", w.n["vS"])
    for i in range(ET):
        w.dma("gpsimd", D["wg_bf"][i * 128:(i + 1) * 128, :], T["wg_t"][i], "gS")
    wu_view = D["wu_sw"].rearrange("(ht p) c -> p ht c", ht=HT)
    for i in range(ET):
        for hf in range(2):
            k = kstage[0]
            buf = T["istage"][k % 2]
            sw = "sWA" if (k % 2 == 0) else "sWB"
            stage_guard(k)
            w.dma("sync", buf, D["wu_i32"][i * 128:(i + 1) * 128,
                                           hf * 1536:(hf + 1) * 1536], sw)
            w.wait("vector", "gS", w.n["gS"])
            w.wait("vector", sw, w.n[sw])
            w.emit("vector", lambda e, buf=buf, hf=hf: e.tensor_copy(
                T["wuconv"][:, hf * 1536:(hf + 1) * 1536], buf), inc="vS")
            cast_after.append(w.n["vS"])
            kstage[0] += 1
        w.wait("gpsimd", "vS", w.n["vS"])
        src = T["wuconv"].rearrange("p (ht j) -> p ht j", ht=HT)
        w.dma("gpsimd", wu_view[:, :, i * 128:(i + 1) * 128], src, "gS")
    wd_view = D["wd_cache"].rearrange("(et p) c -> p et c", et=8)
    for i in range(HT):
        k = kstage[0]
        buf = T["istage"][k % 2]
        sw = "sWA" if (k % 2 == 0) else "sWB"
        stage_guard(k)
        w.dma("sync", buf[:, 0:1024], D["wd_i32"][i * 128:(i + 1) * 128, :], sw)
        w.wait("vector", "gS", w.n["gS"])
        w.wait("vector", sw, w.n[sw])
        w.emit("vector", lambda e, i=i, buf=buf: e.tensor_copy(
            T["stagebf"][i % 2], buf[:, 0:1024]), inc="vS")
        cast_after.append(w.n["vS"])
        kstage[0] += 1
        w.wait("gpsimd", "vS", w.n["vS"])
        src = T["stagebf"][i % 2].rearrange("p (et j) -> p et j", et=8)
        w.dma("gpsimd", wd_view[:, :, i * 128:(i + 1) * 128], src, "gS")
        stbf_free[i % 2] = w.n["gS"]
    wd_cache_ready = w.n["gS"]
    wu_sw_ready = wd_cache_ready
    conv_all_done = w.n["vS"]
    p0_gs = w.n["gS"]

    def ln_rows(fused_double=False, sx=None, sxx=None, rbank=4):
        sx = sx if sx is not None else T["P6s"]
        sxx = sxx if sxx is not None else T["P7s"]
        r4 = PB[rbank][0:1, 0:TC]
        rs1, rs2 = T["rs1"], T["rs2"]
        w.wait("vector", "tS", w.n["tS"])
        w.emit("vector", lambda e, rs1=rs1, sx=sx: e.tensor_scalar_mul(
            rs1, sx, 1.0 / E))
        w.emit("vector", lambda e, rs2=rs2, sxx=sxx: e.tensor_scalar_mul(
            rs2, sxx, 1.0 / E))
        w.emit("vector", lambda e, r4=r4, rs1=rs1: e.scalar_tensor_tensor(
            r4, rs1, -1.0, rs1, OP.mult, OP.mult))
        w.emit("vector", lambda e, r4=r4, rs2=rs2: e.tensor_tensor(
            r4, r4, rs2, OP.add))
        sc = (1.0 + EPS) if fused_double else 1.0
        bi = (EPS * EPS) if fused_double else EPS
        w.emit("vector", lambda e, r4=r4, sc=sc, bi=bi: e.tensor_scalar(
            r4, r4, sc, bi, OP.mult, OP.add), inc="vS")
        w.wait("scalar", "vS", w.n["vS"])
        w.emit("scalar", lambda e, r4=r4, rs2=rs2: e.activation(
            rs2, r4, AF.Sqrt), inc="cS")
        w.wait("vector", "cS", w.n["cS"])
        w.emit("vector", lambda e, r4=r4, rs2=rs2: e.reciprocal(r4, rs2))
        w.emit("vector", lambda e, r4=r4: e.tensor_copy(T["rowA"], r4))
        w.emit("vector", lambda e, r4=r4, rs1=rs1: e.scalar_tensor_tensor(
            T["rowB"], rs1, -1.0, r4, OP.mult, OP.mult), inc="vS")

    def ln_apply(xbset=None, ba=6, bb=7):
        xbset = xbset if xbset is not None else T["xb"]
        w.wait("tensor", "vS", w.n["vS"])
        w.emit("tensor", lambda e, ba=ba: e.matmul(
            P(ba), T["ones_row"], T["rowA"], start=True, stop=True), inc="tS")
        w.emit("tensor", lambda e, bb=bb: e.matmul(
            P(bb), T["ones_row"], T["rowB"], start=True, stop=True), inc="tS")
        w.wait("vector", "tS", w.n["tS"])
        for i in range(ET):
            w.emit("vector", lambda e, i=i, xbset=xbset, ba=ba: e.tensor_tensor(
                xbset[i], xbset[i], P(ba), OP.mult))
            w.emit("vector", lambda e, i=i, xbset=xbset, bb=bb: e.tensor_tensor(
                xbset[i], xbset[i], P(bb), OP.add),
                inc="vS" if i == ET - 1 else None)

    prev_free = {"wu": (0, 0), "wd": (0, 0)}

    def ffn_body(res_tiles, store_fn, res_prewait=None, mid_hook=None,
                 ln_wait=None, top_hook=None, entry_cs=None, entry_gs=True,
                 ht_hook=None):
        ln_apply_done = w.n["vS"] if ln_wait is None else ln_wait
        if top_hook is not None:
            top_hook()
        silu_sem = {}
        h_sem = {}
        w.wait("tensor", "vS", ln_apply_done)
        w.wait("tensor", "cS", w.n["cS"] if entry_cs is None else entry_cs)
        if entry_gs:
            w.wait("vector", "gS", w.n["gS"])
        up_mm_hist = {}
        wu_ld_v = {}

        def issue_wu(ht):
            wub = T["wubuf"][ht % 2]
            su = "sUA" if (ht % 2 == 0) else "sUB"
            if ht >= 2:
                w.wait("sync", "tS", up_mm_hist[ht - 2])
            elif prev_free["wu"] == (0, 0):
                w.wait("sync", "gS", wu_sw_ready)
            else:
                w.wait("sync", "tS", prev_free["wu"][ht])
            w.dma("sync", wub, D["wu_sw"][ht * 128:(ht + 1) * 128, :], su)
            wu_ld_v[ht] = w.n[su]

        issue_wu(0)
        issue_wu(1)
        for ht in range(HT):
            gb = P(0 + (ht % 2))
            ub_ = P(2 + (ht % 2))
            wub = T["wubuf"][ht % 2]
            su = "sUA" if (ht % 2 == 0) else "sUB"
            if ht >= 2:
                w.wait("tensor", "cS", silu_sem[ht - 2])
                w.wait("tensor", "vS", h_sem[ht - 2])
            for ec in range(ET):
                w.emit("tensor", lambda e, ht=ht, ec=ec, gb=gb: e.matmul(
                    gb, T["wg_t"][ec][:, ht * 128:(ht + 1) * 128], T["xb"][ec],
                    start=(ec == 0), stop=(ec == ET - 1)),
                    inc="tS" if ec == ET - 1 else None)
            gate_mm = w.n["tS"]
            w.wait("tensor", su, wu_ld_v[ht])
            for ec in range(ET):
                w.emit("tensor", lambda e, ht=ht, ec=ec, ub_=ub_, wub=wub: e.matmul(
                    ub_, wub[:, ec * 128:(ec + 1) * 128], T["xb"][ec],
                    start=(ec == 0), stop=(ec == ET - 1)),
                    inc="tS" if ec == ET - 1 else None)
            up_mm = w.n["tS"]
            up_mm_hist[ht] = up_mm
            if ht + 2 < HT:
                issue_wu(ht + 2)
            w.wait("scalar", "tS", gate_mm)
            w.emit("scalar", lambda e, ht=ht, gb=gb: e.activation(
                T["silu"][ht % 2], gb, AF.Silu, scale=T["c_qdeq"]), inc="cS")
            silu_sem[ht] = w.n["cS"]
            w.wait("vector", "tS", up_mm)
            w.wait("vector", "cS", silu_sem[ht])
            w.emit("vector", lambda e, ht=ht, ub_=ub_: e.scalar_tensor_tensor(
                T["h_t"][ht], ub_, QDEQ, T["silu"][ht % 2], OP.mult, OP.mult),
                inc="vS")
            h_sem[ht] = w.n["vS"]
            if ht_hook is not None:
                ht_hook(ht)
        if mid_hook is not None:
            mid_hook(silu_sem[HT - 1], h_sem[HT - 1])
        dn_sem = {}
        res_sem = {}
        wd_ld_v = {}

        def issue_wd(et):
            buf = T["wdbuf"][et % 2]
            sw = "sWA" if (et % 2 == 0) else "sWB"
            if et >= 2:
                w.wait("sync", "tS", dn_sem[et - 2])
            elif prev_free["wd"] == (0, 0):
                w.wait("sync", "gS", wd_cache_ready)
            else:
                w.wait("sync", "tS", prev_free["wd"][et])
            w.dma("sync", buf, D["wd_cache"][et * 128:(et + 1) * 128, :], sw)
            wd_ld_v[et] = w.n[sw]

        issue_wd(0)
        issue_wd(1)
        for et in range(ET):
            db = P(4 + (et % 2))
            buf = T["wdbuf"][et % 2]
            sw = "sWA" if (et % 2 == 0) else "sWB"
            if et >= 2:
                w.wait("tensor", "vS", res_sem[et - 2])
            w.wait("tensor", sw, wd_ld_v[et])
            w.wait("tensor", "vS", h_sem[HT - 1])
            for ht in range(HT):
                w.emit("tensor", lambda e, ht=ht, db=db, buf=buf: e.matmul(
                    db, buf[:, ht * 128:(ht + 1) * 128], T["h_t"][ht],
                    start=(ht == 0), stop=(ht == HT - 1)),
                    inc="tS" if ht == HT - 1 else None)
            dn_sem[et] = w.n["tS"]
            if et + 2 < ET:
                issue_wd(et + 2)
            if res_prewait is not None:
                res_prewait(et)
            w.wait("vector", "tS", dn_sem[et])
            w.emit("vector", lambda e, et=et, db=db: e.scalar_tensor_tensor(
                res_tiles[et], db, 0.5 * QDEQ, res_tiles[et], OP.mult, OP.add),
                inc="vS")
            res_sem[et] = w.n["vS"]
            store_fn(et)
        prev_free["wu"] = (up_mm_hist[HT - 2], up_mm_hist[HT - 1])
        prev_free["wd"] = (dn_sem[ET - 2], dn_sem[ET - 1])
        return silu_sem[HT - 1]

    if PHASES == "p0":
        finish_early()
        return

    x1_cast_done = {}
    x1_store_done = {}
    prev_sq = {}
    prev_stats = {}
    st2_prev = {}
    ln_ready = {}
    xh_vec_done = {}
    last_silu = [None]

    ln1_sq = {}
    ln1_mm_done = {}

    ln1_ld = {}

    def ln1_load_issue(tc_i, i):
        c0 = tc_i * TC
        buf = T["xf0"] if (i % 2 == 0) else T["xf1"]
        sx = "sXA" if (i % 2 == 0) else "sXB"
        if i >= 2:
            w.wait("sync", "cS", ln1_sq[(tc_i, i - 2)])
        elif tc_i > 0:
            w.wait("sync", "cS", prev_sq[tc_i - 1])
        elif tc_i == 0:
            w.wait("sync", "vS", xf_free)
        w.dma("sync", buf, D["x"][i * 128:(i + 1) * 128, c0:c0 + TC], sx)
        ln1_ld[(tc_i, i)] = (sx, w.n[sx])

    def ln1_pair(tc_i, i):
        buf = T["xf0"] if (i % 2 == 0) else T["xf1"]
        xbset = T["xb0"] if (tc_i % 2 == 0) else T["xb1"]
        if i == 0:
            sHp = "sHA" if (tc_i % 2 == 0) else "sHB"
            if tc_i >= 2:
                w.wait("scalar", sHp, w.n[sHp])
                w.wait("scalar", "tS", prev_stats[tc_i - 2])
            if tc_i >= 1:
                w.wait("scalar", "tS", ln1_mm_done[tc_i - 1])
        sx, v = ln1_ld[(tc_i, i)]
        w.wait("scalar", sx, v)
        w.emit("scalar", lambda e, i=i, buf=buf, xbset=xbset: e.copy(
            xbset[i], buf), inc="cS")
        w.emit("scalar", lambda e, i=i, buf=buf: e.activation(
            T["sqs"][i], buf, AF.Square), inc="cS")
        ln1_sq[(tc_i, i)] = w.n["cS"]
        if i == ET - 1:
            prev_sq[tc_i] = w.n["cS"]

    def emit_ln1_loads(tc_i):
        for i in range(ET):
            ln1_load_issue(tc_i, i)
            ln1_pair(tc_i, i)

    def emit_ln1_mms(tc_i, bSx, bSxx, brr, pre_waits=()):
        xbset = T["xb0"] if (tc_i % 2 == 0) else T["xb1"]
        for (eng_, sem_, val_) in pre_waits:
            w.wait(eng_, sem_, val_)
        if tc_i > 0:
            w.wait("tensor", "vS", w.n["vS"])
        for i in range(ET):
            w.wait("tensor", "cS", ln1_sq[(tc_i, i)])
            w.emit("tensor", lambda e, i=i, bSx=bSx, xbset=xbset: e.matmul(
                PB[bSx][0:1, 0:TC], T["ones_col"], xbset[i],
                start=(i == 0), stop=(i == ET - 1)), inc="tS")
            w.emit("tensor", lambda e, i=i, bSxx=bSxx: e.matmul(
                PB[bSxx][0:1, 0:TC], T["ones_col"], T["sqs"][i],
                start=(i == 0), stop=(i == ET - 1)), inc="tS")
        ln1_mm_done[tc_i] = w.n["tS"]
        ln_rows(fused_double=False, sx=PB[bSx][0:1, 0:TC], sxx=PB[bSxx][0:1, 0:TC],
                rbank=brr)
        if tc_i == 0:
            w.wait("tensor", "vS", conv_all_done)
        ln_apply(xbset=xbset, ba=bSx, bb=bSxx)
        ln_ready[tc_i] = w.n["vS"]

    def emit_ln1(tc_i, bSx, bSxx, brr, pre_waits=()):
        emit_ln1_loads(tc_i)
        emit_ln1_mms(tc_i, bSx, bSxx, brr, pre_waits)

    for tc_i in range(NCH):
        c0 = tc_i * TC
        T["xb"] = T["xb0"] if (tc_i % 2 == 0) else T["xb1"]
        sHp = "sHA" if (tc_i % 2 == 0) else "sHB"
        if tc_i == 0:
            emit_ln1(0, 6, 7, 4)

        xres = {et: (T["xf2"] if (et % 2 == 0) else T["xf3"]) for et in range(ET)}
        resld = {}

        def issue_reload(et, tc_i=tc_i, c0=c0):
            sr = "sRA" if (et % 2 == 0) else "sRB"
            sg = "sGA" if (et % 2 == 0) else "sGB"
            if et >= 2:
                w.wait("sync", "cS", x1_cast_done[(tc_i, et - 2)])
                w.wait("sync", sg, w.n[sg])
            elif tc_i > 0:
                w.wait("sync", "cS", x1_cast_done[(tc_i - 1, ET - 2 + et)])
                w.wait("sync", sg, w.n[sg])
            w.dma("sync", xres[et], D["x"][et * 128:(et + 1) * 128, c0:c0 + TC], sr)
            resld[et] = (sr, w.n[sr])

        issue_reload(0)
        issue_reload(1)

        def res_prewait(et, resld=resld):
            sr, v = resld[et]
            w.wait("vector", sr, v)

        st2_done = {}

        def emit_st2(j, tc_i=tc_i):
            if j == 0:
                w.wait("tensor", "vS", xh_vec_done.get(tc_i - 1, 0))
            w.wait("tensor", "cS", x1_cast_done[(tc_i, j)])
            w.emit("tensor", lambda e, j=j: e.matmul(
                T["P6s"], T["ones_col"], T["xb"][j],
                start=(j == 0), stop=(j == ET - 1)), inc="tS")
            w.emit("tensor", lambda e, j=j: e.matmul(
                T["P7s"], T["ones_col"], T["sqs2"][j % 2],
                start=(j == 0), stop=(j == ET - 1)), inc="tS")
            st2_done[j] = w.n["tS"]
            st2_prev[(tc_i, j)] = w.n["tS"]
            if j == ET - 1:
                prev_stats[tc_i] = w.n["tS"]

        def store_x1(et, tc_i=tc_i, c0=c0, xres=xres):
            sg = "sGA" if (et % 2 == 0) else "sGB"
            w.wait("gpsimd", "vS", w.n["vS"])
            w.dma("gpsimd", D["x1"][et * 128:(et + 1) * 128, c0:c0 + TC],
                  xres[et], sg)
            x1_store_done[(tc_i, et)] = w.n[sg]
            if et >= 2:
                emit_st2(et - 2)
            w.wait("scalar", "vS", w.n["vS"])
            if et >= 2:
                w.wait("scalar", "tS", st2_done[et - 2])
            elif tc_i > 0:
                w.wait("scalar", "tS", st2_prev[(tc_i - 1, ET - 2 + et)])
            w.emit("scalar", lambda e, et=et, xres=xres: e.copy(
                T["xb"][et], xres[et]), inc="cS")
            w.emit("scalar", lambda e, et=et, xres=xres: e.activation(
                T["sqs2"][et % 2], xres[et], AF.Square), inc="cS")
            x1_cast_done[(tc_i, et)] = w.n["cS"]
            if et == ET - 1:
                emit_st2(ET - 2)
                emit_st2(ET - 1)
            if et + 2 < ET:
                issue_reload(et + 2)

        def p1_top(tc_i=tc_i):
            if tc_i + 1 < NCH:
                ln1_load_issue(tc_i + 1, 0)
                ln1_load_issue(tc_i + 1, 1)

        def p1_ht(ht, tc_i=tc_i):
            if tc_i + 1 < NCH and ht % 3 == 0 and ht // 3 < ET:
                i = ht // 3
                ln1_pair(tc_i + 1, i)
                if i + 2 < ET:
                    ln1_load_issue(tc_i + 1, i + 2)

        def p1_hook(silu_last, h_last, tc_i=tc_i):
            if tc_i + 1 < NCH:
                emit_ln1_mms(tc_i + 1, 0, 1, 2,
                             pre_waits=(("tensor", "cS", silu_last),
                                        ("tensor", "vS", h_last)))

        last_silu[0] = ffn_body(
            xres, store_x1, res_prewait, mid_hook=p1_hook,
            ln_wait=ln_ready[tc_i], top_hook=p1_top, ht_hook=p1_ht,
            entry_cs=last_silu[0], entry_gs=(tc_i == 0))

        ln_rows(fused_double=True)
        w.wait("tensor", "vS", w.n["vS"])
        w.emit("tensor", lambda e: e.matmul(
            P(6), T["ones_row"], T["rowA"], start=True, stop=True), inc="tS")
        w.emit("tensor", lambda e: e.matmul(
            P(7), T["ones_row"], T["rowB"], start=True, stop=True), inc="tS")
        w.wait("vector", "tS", w.n["tS"])
        xh_cs = {}
        for i in range(ET):
            w.emit("vector", lambda e, i=i: e.tensor_tensor(
                T["xb"][i], T["xb"][i], P(6), OP.mult))
            w.emit("vector", lambda e, i=i: e.tensor_tensor(
                T["xb"][i], T["xb"][i], P(7), OP.add), inc="vS")
            w.wait("scalar", "vS", w.n["vS"])
            w.emit("scalar", lambda e, i=i: e.activation(
                T["xb"][i], T["xb"][i], AF.Identity,
                scale=T["gamma_col"][:, i:i + 1],
                bias=T["beta_col"][:, i:i + 1]), inc="cS")
            xh_cs[i] = w.n["cS"]
            w.wait("gpsimd", "cS", xh_cs[i])
            w.dma("gpsimd", D["xh_own"][i * 128:(i + 1) * 128, c0:c0 + TC],
                  T["xb"][i], sHp)
        xh_vec_done[tc_i] = w.n["vS"]
        cbufs = T["cb"]
        w.wait("vector", "cS", xh_cs[ET - 1])
        w.wait("vector", sHp, w.n[sHp])
        for j in range(4):
            dst = cbufs[j]
            w.emit("vector", lambda e, j=j, dst=dst: e.tensor_scalar_mul(
                dst, T["xb"][j], T["selB"]))
            w.emit("vector", lambda e, j=j, dst=dst: e.scalar_tensor_tensor(
                dst, T["xb"][j + 4], T["selA"], dst, OP.mult, OP.add), inc="vS")
            w.wait("gpsimd", "vS", w.n["vS"])
            w.dma("gpsimd",
                  D["cc1_in"][tc_i * 512 + j * 128:tc_i * 512 + (j + 1) * 128, :],
                  dst, sHp)
        w.wait("gpsimd", sHp, w.n[sHp])
        w.emit("gpsimd", lambda e, tc_i=tc_i: e.collective_compute(
            "AllGather", OP.bypass,
            replica_groups=[[0, 1], [2, 3], [4, 5], [6, 7]],
            ins=[D["cc1_in"][tc_i * 512:(tc_i + 1) * 512, :]],
            outs=[D["cc1_out"][tc_i * 1024:(tc_i + 1) * 1024, :]]), inc="cc")

    if PHASES == "p1":
        finish_early()
        return

    w.wait("gpsimd", "cc", w.n["cc"])
    w.barrier()

    if PHASES == "cc1":
        finish_early()
        return

    for hl in range(4):
        r = hl * 128
        if hl > 0:
            w.wait("sync", "vS", w.n["vS"])
        w.dma("sync", T["xhh"][hl][:, 0:TOK], D["xh_own"][r:r + 128, :], "sH")
        w.dma("sync", T["xsc"][:, 0:TOK], D["xh_own"][512 + r:512 + r + 128, :],
              "sH")
        for tcj in range(NCH):
            base = tcj * 1024
            w.dma("sync", T["xhh"][hl][:, TOK + tcj * TC:TOK + (tcj + 1) * TC],
                  D["cc1_out"][base + 512 + r:base + 512 + r + 128, :], "sH")
            w.dma("sync", T["xsc"][:, TOK + tcj * TC:TOK + (tcj + 1) * TC],
                  D["cc1_out"][base + r:base + r + 128, :], "sH")
        w.wait("vector", "sH", w.n["sH"])
        w.emit("vector", lambda e, hl=hl: e.tensor_scalar_mul(
            T["xhh"][hl], T["xhh"][hl], T["selA"]))
        w.emit("vector", lambda e, hl=hl: e.scalar_tensor_tensor(
            T["xhh"][hl], T["xsc"], T["selB"], T["xhh"][hl], OP.mult, OP.add),
            inc="vS")
    w.emit("vector", lambda e: e.memset(T["Vt"][:, 128:KC * 129:129], 1.0), inc="vS")
    xhh_ready = w.n["vS"]
    if P2LVL == 1:
        finish_early()
        return

    tblld = {}
    uv_ready = {}

    def prep_head(hl):
        Qu, Qv, Kb, Vt = T["Qu"], T["Qv"], T["Kb"], T["Vt"]
        UVlo, UVup, ABt = T["UVlo"], T["UVup"], T["AB"]
        w.wait("sync", "vS", w.n["vS"])
        w.dma("sync", T["pca"], D["pca_lo_h"][hl], "sT")
        w.dma("sync", T["psa"], D["psa_lo_h"][hl], "sT")
        w.dma("sync", T["pe_hi"], D["pe_hi_h"][hl], "sT")
        tbl_ld = w.n["sT"]
        tblld[hl] = tbl_ld
        w.wait("tensor", "vS", xhh_ready)
        w.wait("tensor", "cS", w.n["cS"])
        bfree = {4: 0, 5: 0, 6: 0, 7: 0}

        def pp(t):
            return 4 + (t % 4)

        for t in range(QT):
            cs = slice(t * 512, t * 512 + 512)
            bk_ = pp(t)
            w.wait("tensor", "cS", bfree[bk_])
            w.emit("tensor", lambda e, cs=cs, hl=hl, bk_=bk_: e.matmul(
                P(bk_), T["wq"], T["xhh"][hl][:, cs], start=True, stop=True),
                inc="tS")
            w.wait("scalar", "tS", w.n["tS"])
            w.emit("scalar", lambda e, cs=cs, hl=hl, Qu=Qu, bk_=bk_: e.activation(
                Qu[:, cs], P(bk_), AF.Identity, bias=T["quv"][:, hl:hl + 1]))
            w.emit("scalar", lambda e, cs=cs, hl=hl, Qv=Qv, bk_=bk_: e.activation(
                Qv[:, cs], P(bk_), AF.Identity, bias=T["quv"][:, 4 + hl:5 + hl]),
                inc="cS")
            bfree[bk_] = w.n["cS"]
        for t in range(QT):
            cs = slice(t * 512, t * 512 + 512)
            bk_ = pp(t)
            w.wait("tensor", "cS", bfree[bk_])
            w.emit("tensor", lambda e, cs=cs, hl=hl, bk_=bk_: e.matmul(
                P(bk_), T["wk"], T["xhh"][hl][:, cs], start=True, stop=True),
                inc="tS")
            w.wait("scalar", "tS", w.n["tS"])
            w.emit("scalar", lambda e, cs=cs, Kb=Kb, bk_=bk_: e.activation(
                Kb[:, cs], P(bk_), AF.Identity, bias=T["bk"]), inc="cS")
            bfree[bk_] = w.n["cS"]
        for g in range(8):
            bk_ = pp(g)
            w.wait("tensor", "cS", bfree[bk_])
            for s4 in range(4):
                ck = g * 4 + s4
                w.emit("tensor", lambda e, ck=ck, s4=s4, hl=hl, bk_=bk_: e.matmul(
                    P(bk_)[:, s4 * 128:s4 * 128 + 128],
                    T["xhh"][hl][:, ck * 128:ck * 128 + 128], T["wv"],
                    start=True, stop=True), inc="tS")
            w.wait("scalar", "tS", w.n["tS"])
            out_ap = Vt.rearrange("p (t c) -> p t c", c=129)[:, g * 4:g * 4 + 4, 0:128]
            w.emit("scalar", lambda e, out_ap=out_ap, bk_=bk_: e.copy(
                out_ap, P(bk_)), inc="cS")
            bfree[bk_] = w.n["cS"]
        for t in range(QT):
            cs = slice(t * 512, t * 512 + 512)
            bk_ = pp(t)
            w.wait("tensor", "cS", bfree[bk_])
            w.emit("tensor", lambda e, cs=cs, Qv=Qv, bk_=bk_: e.matmul(
                P(bk_), T["wab"], Qv[:, cs], start=True, stop=True), inc="tS")
            w.wait("scalar", "tS", w.n["tS"])
            w.emit("scalar", lambda e, cs=cs, ABt=ABt, bk_=bk_: e.copy(
                ABt[:, cs], P(bk_)), inc="cS")
            bfree[bk_] = w.n["cS"]
        w.wait("vector", "cS", w.n["cS"])
        w.wait("vector", "tS", w.n["tS"])
        w.wait("vector", "sT", tbl_ld)

        ABs = T["ABswap"]
        w.wait("sync", "cS", w.n["cS"])
        w.wait("sync", "tS", w.n["tS"])
        w.dma("sync", ABs[0:64, :], ABt[64:128, :], "sSW")
        w.dma("sync", ABs[64:128, :], ABt[0:64, :], "sSW")
        ab_swap = w.n["sSW"]

        def build_uv(UV, pca, psa, shift):
            ncc = S - shift
            w.emit("vector", lambda e: e.tensor_tensor(
                T["T1"][:, 0:ncc], ABt[:, shift:S], pca[:, 0:ncc], OP.mult))
            w.emit("vector", lambda e: e.tensor_tensor(
                T["T2"][:, 0:ncc], ABs[:, shift:S], psa[:, 0:ncc], OP.mult),
                inc="vS")
            w.emit("vector", lambda e: e.tensor_tensor(
                UV[0:64, 0:ncc], T["T1"][0:64, 0:ncc], T["T2"][0:64, 0:ncc],
                OP.add))
            w.emit("vector", lambda e: e.tensor_sub(
                UV[64:128, 0:ncc], T["T1"][64:128, 0:ncc], T["T2"][64:128, 0:ncc]),
                inc="vS")
            if shift:
                w.emit("vector", lambda e: e.memset(UV[:, S - shift:S], 0.0),
                       inc="vS")

        w.wait("vector", "sSW", ab_swap)
        build_uv(UVlo, T["pca"], T["psa"], 0)
        w.wait("sync", "vS", w.n["vS"])
        w.dma("sync", T["pca"], D["pca_up_h"][hl], "sT")
        w.dma("sync", T["psa"], D["psa_up_h"][hl], "sT")
        w.wait("vector", "sT", w.n["sT"])
        build_uv(UVup, T["pca"], T["psa"], 1)
        uv_ready[hl] = w.n["vS"]

    def emit_bound():
        Qu, Kb, UVlo, UVup = T["Qu"], T["Kb"], T["UVlo"], T["UVup"]
        w.emit("vector", lambda e, Qu=Qu: e.reduce_max(
            T["mx"][:, 0:1], Qu, axis=AX.X, apply_absolute_value=True))
        w.emit("vector", lambda e, Kb=Kb: e.reduce_max(
            T["mx"][:, 1:2], Kb, axis=AX.X, apply_absolute_value=True))
        w.emit("vector", lambda e, UVlo=UVlo: e.reduce_max(
            T["mx"][:, 2:3], UVlo, axis=AX.X, apply_absolute_value=True))
        w.emit("vector", lambda e, UVup=UVup: e.reduce_max(
            T["mx"][:, 3:4], UVup, axis=AX.X, apply_absolute_value=True))
        w.emit("vector", lambda e: e.tensor_tensor(
            T["mx"][:, 4:5], T["mx"][:, 0:1], T["mx"][:, 1:2], OP.mult))
        w.emit("vector", lambda e: e.tensor_tensor(
            T["mx"][:, 5:6], T["mx"][:, 2:3], T["mx"][:, 3:4], OP.max))
        w.emit("vector", lambda e: e.tensor_copy(T["mxb"], T["mx"][:, 4:6]), inc="vS")
        w.wait("tensor", "vS", w.n["vS"])
        w.emit("tensor", lambda e: e.matmul(
            P(7)[0:1, 0:2], T["ones_col"], T["mxb"], start=True, stop=True), inc="tS")
        w.wait("vector", "tS", w.n["tS"])
        w.emit("vector", lambda e: e.tensor_copy(T["rs1"][0:1, 0:2], P(7)[0:1, 0:2]))
        w.emit("vector", lambda e: e.tensor_tensor(
            T["rs1"][0:1, 2:3], T["rs1"][0:1, 0:1], T["rs1"][0:1, 1:2], OP.add))
        w.emit("vector", lambda e: e.tensor_scalar(
            T["r_b"], T["rs1"][0:1, 2:3], -ISQ * 1.02, -0.02, OP.mult, OP.add),
            inc="vS")
        w.wait("tensor", "vS", w.n["vS"])
        w.emit("tensor", lambda e: e.matmul(
            P(7)[:, 8:9], T["ones_row"], T["r_b"], start=True, stop=True), inc="tS")
        w.wait("scalar", "tS", w.n["tS"])
        w.emit("scalar", lambda e: e.copy(T["negb"], P(7)[:, 8:9]), inc="cS")

    def head_scores(hl):
        Qu, Kb, Vt = T["Qu"], T["Kb"], T["Vt"]
        UVlo, UVup, pehi, negb = T["UVlo"], T["UVup"], T["pe_hi"], T["negb"]
        w.wait("tensor", "sT", tblld[hl])
        w.wait("tensor", "vS", uv_ready[hl])
        pv_done = {}
        exp_done = {}
        pend = [None]

        def evac_a(qt):
            w.wait("vector", "tS", w.n["tS"])
            for qs in range(4):
                w.emit("vector", lambda e, qs=qs: e.tensor_copy(
                    T["Ssum"][:, qs:qs + 1],
                    PB[4 + qs // 2][:, (qs % 2) * 129 + 128:(qs % 2) * 129 + 129]))
            w.emit("vector", lambda e: e.reciprocal(T["invS"], T["Ssum"]), inc="vS")
            w.wait("scalar", "vS", w.n["vS"])
            if pend[0] is not None:
                w.wait("scalar", "tS", pend[0][1])
            for qs in range(4):
                w.emit("scalar", lambda e, qs=qs: e.activation(
                    T["ctx_tm4"][qs],
                    PB[4 + qs // 2][:, (qs % 2) * 129:(qs % 2) * 129 + 128],
                    AF.Identity, scale=T["invS"][:, qs:qs + 1]), inc="cS")
            return w.n["cS"]

        def evac_b(hl, qt, evac_cs):
            w.wait("tensor", "cS", evac_cs)
            for qs in range(4):
                w.emit("tensor", lambda e, qs=qs: e.matmul(
                    P(7)[:, qs * 128:qs * 128 + 128],
                    T["ctx_tm4"][qs], T["ident"], start=True, stop=True), inc="tS")
            tp_done = w.n["tS"]
            w.wait("scalar", "tS", tp_done)
            for qs in range(4):
                w.emit("scalar", lambda e, hl=hl, qt=qt, qs=qs: e.activation(
                    T["ctxf"][hl][:, qt * 512 + qs * 128:qt * 512 + qs * 128 + 128],
                    P(7)[:, qs * 128:qs * 128 + 128],
                    AF.Identity, bias=T["bv"]), inc="cS")
            return tp_done

        def emit_pv(qt, c):
            ci = qt * KC + c
            w.wait("tensor", "cS", exp_done[ci])
            for qs in range(4):
                ctxap = PB[4 + qs // 2][:, (qs % 2) * 129:(qs % 2) * 129 + 129]
                w.emit("tensor", lambda e, c=c, qs=qs, ci=ci, Vt=Vt, ctxap=ctxap:
                       e.matmul(
                    ctxap,
                    T["Pt"][:, (ci % 4) * 512 + qs * 128:
                            (ci % 4) * 512 + qs * 128 + 128],
                    Vt[:, c * 129:c * 129 + 129],
                    start=(c == 0 and (qs % 2) == 0),
                    stop=(c == KC - 1 and (qs % 2) == 1)),
                    inc="tS" if (qs == 3 or c == KC - 1) else None)
            pv_done[ci] = w.n["tS"]

        for qt in range(QT):
            q0 = qt * 512
            for c in range(KC):
                ci = qt * KC + c
                sb = P(ci % 4)
                d0 = c * 128 - q0
                a = max(0, min(512, d0 - 1))
                b2 = max(0, min(512, d0 + 127))
                if ci >= 4:
                    w.wait("tensor", "cS", exp_done[ci - 4])
                mm_list = [lambda e, st, c=c, q0=q0, sb=sb, Qu=Qu, Kb=Kb: e.matmul(
                    sb, Kb[:, c * 128:c * 128 + 128], Qu[:, q0:q0 + 512],
                    start=True, stop=st)]
                if b2 < 512:
                    mm_list.append(
                        lambda e, st, c=c, q0=q0, sb=sb, b2=b2, UVlo=UVlo, pehi=pehi:
                        e.matmul(sb[:, b2:512], pehi[:, c * 128:c * 128 + 128],
                                 UVlo[:, q0 + b2:q0 + 512], start=False, stop=st))
                if a > 0:
                    mm_list.append(
                        lambda e, st, c=c, q0=q0, sb=sb, a=a, UVup=UVup, pehi=pehi:
                        e.matmul(sb[:, 0:a], pehi[:, c * 128:c * 128 + 128],
                                 UVup[:, q0:q0 + a], start=False, stop=st))
                for mi, mfn in enumerate(mm_list):
                    last = mi == len(mm_list) - 1
                    w.emit("tensor",
                           lambda e, mfn=mfn, last=last: mfn(e, last),
                           inc="tS" if last else None)
                if c == 2 and pend[0] is not None:
                    pqt, _, pcs = pend[0]
                    pend[0] = (pqt, evac_b(hl, pqt, pcs), pcs)
                mm_here = w.n["tS"]
                vec_here = w.n["vS"]
                if a < b2 and not NOSTRIP:
                    wdt = b2 - a
                    moff = a - (d0 - 1)
                    w.wait("tensor", "vS", w.n["vS"])
                    w.emit("tensor",
                           lambda e, c=c, q0=q0, a=a, wdt=wdt, UVlo=UVlo, pehi=pehi:
                           e.matmul(P(6)[:, 0:wdt], pehi[:, c * 128:c * 128 + 128],
                                    UVlo[:, q0 + a:q0 + a + wdt],
                                    start=True, stop=True), inc="tS")
                    w.emit("tensor",
                           lambda e, c=c, q0=q0, a=a, wdt=wdt, UVup=UVup, pehi=pehi:
                           e.matmul(P(6)[:, 128:128 + wdt],
                                    pehi[:, c * 128:c * 128 + 128],
                                    UVup[:, q0 + a:q0 + a + wdt],
                                    start=True, stop=True), inc="tS")
                    mm_here = w.n["tS"]
                    w.wait("vector", "tS", mm_here)
                    w.emit("vector", lambda e, wdt=wdt, moff=moff: e.tensor_tensor(
                        T["strip"][:, 0:wdt], P(6)[:, 0:wdt],
                        T["m_lo"][:, moff:moff + wdt], OP.mult))
                    w.emit("vector", lambda e, wdt=wdt, moff=moff: e.tensor_tensor(
                        T["strip"][:, 128:128 + wdt], P(6)[:, 128:128 + wdt],
                        T["m_up"][:, moff:moff + wdt], OP.mult))
                    w.emit("vector", lambda e, wdt=wdt: e.tensor_tensor(
                        T["strip"][:, 0:wdt], T["strip"][:, 0:wdt],
                        T["strip"][:, 128:128 + wdt], OP.add))
                    w.emit("vector", lambda e, sb=sb, a=a, wdt=wdt: e.tensor_tensor(
                        sb[:, a:a + wdt], sb[:, a:a + wdt], T["strip"][:, 0:wdt],
                        OP.add), inc="vS")
                    vec_here = w.n["vS"]
                if qt == 0 and c == 0:
                    emit_bound()
                if c >= 2 and HSLVL >= 2:
                    emit_pv(qt, c - 2)
                w.wait("scalar", "tS", mm_here)
                w.wait("scalar", "vS", vec_here)
                if ci >= 4 and (ci - 4) in pv_done:
                    w.wait("scalar", "tS", pv_done[ci - 4])
                if HSLVL >= 1:
                    w.emit("scalar", lambda e, ci=ci, sb=sb, negb=negb: e.activation(
                        T["Pt"][:, (ci % 4) * 512:(ci % 4) * 512 + 512], sb,
                        AF.Exp, bias=negb, scale=T["c_isq"]), inc="cS")
                else:
                    w.emit("scalar", lambda e: e.nop(), inc="cS")
                exp_done[ci] = w.n["cS"]
            if HSLVL >= 2:
                emit_pv(qt, KC - 2)
                emit_pv(qt, KC - 1)
            if HSLVL < 3:
                continue
            ecs = evac_a(qt)
            if qt < QT - 1:
                pend[0] = (qt, None, ecs)
            else:
                evac_b(hl, qt, ecs)
                pend[0] = None
                w.wait("tensor", "cS", w.n["cS"])
        w.wait("vector", "cS", w.n["cS"])
        w.wait("vector", "gS", w.n["gS"])
        xs = T["xsc"]
        w.emit("vector", lambda e, hl=hl, xs=xs: e.tensor_scalar_mul(
            xs[:, 0:TOK], T["ctxf"][hl][:, 0:TOK], T["selB"]))
        w.emit("vector", lambda e, hl=hl, xs=xs: e.scalar_tensor_tensor(
            xs[:, 0:TOK], T["ctxf"][hl][:, TOK:S], T["selA"], xs[:, 0:TOK],
            OP.mult, OP.add))
        w.emit("vector", lambda e, hl=hl, xs=xs: e.tensor_scalar_mul(
            xs[:, TOK:S], T["ctxf"][hl][:, 0:TOK], T["selA"]))
        w.emit("vector", lambda e, hl=hl, xs=xs: e.scalar_tensor_tensor(
            xs[:, TOK:S], T["ctxf"][hl][:, TOK:S], T["selB"], xs[:, TOK:S],
            OP.mult, OP.add), inc="vS")
        w.wait("gpsimd", "vS", w.n["vS"])
        w.dma("gpsimd", D["cc2_in"][hl * 128:(hl + 1) * 128, :], xs[:, 0:TOK], "gS")
        w.dma("gpsimd", D["ctx_own"][hl * 128:(hl + 1) * 128, :], xs[:, TOK:S], "gS")
        w.wait("gpsimd", "gS", w.n["gS"])
        w.emit("gpsimd", lambda e, hl=hl: e.collective_compute(
            "AllGather", OP.bypass,
            replica_groups=[[0, 1], [2, 3], [4, 5], [6, 7]],
            ins=[D["cc2_in"][hl * 128:(hl + 1) * 128, :]],
            outs=[D["cc2_out"][hl * 256:(hl + 1) * 256, :]]), inc="cc")

    for hl in range(4):
        prep_head(hl)
        if P2LVL == 2 and hl == 0:
            finish_early()
            return
        head_scores(hl)
        if P2LVL == 3 and hl == 0:
            finish_early()
            return

    if PHASES == "p2":
        finish_early()
        return

    w.wait("gpsimd", "cc", w.n["cc"])
    w.barrier()

    if PHASES == "cc2":
        finish_early()
        return

    for i in range(ET):
        w.dma("sync", T["wo_sb"][i], D["wo_bf"][i * 128:(i + 1) * 128, :], "sS")
    wts_ld = w.n["sS"]

    out_store = {}
    wo_mm_done = {}
    for tc_i in range(NCH):
        c0 = tc_i * TC
        T["xb"] = T["xb0"] if (tc_i % 2 == 0) else T["xb1"]
        ctx_sel_last = 0
        for i in range(ET):
            if tc_i > 0 and i == 0:
                w.wait("sync", "tS", wo_mm_done[tc_i - 1])
            if i > 0 or tc_i > 0:
                w.wait("sync", "vS", w.n["vS"])
            w.dma("sync", T["ctxc"][i],
                  D["ctx_own"][(i % 4) * 128:(i % 4) * 128 + 128, c0:c0 + TC], "sC")
            rb = 256 * (i % 4) + 128 * (i // 4)
            w.dma("sync", T["csc"],
                  D["cc2_out"][rb:rb + 128, c0:c0 + TC], "sC")
            w.wait("vector", "sC", w.n["sC"])
            w.emit("vector", lambda e, i=i: e.tensor_scalar_mul(
                T["ctxc"][i], T["ctxc"][i], T["selT"][:, i:i + 1]))
            w.emit("vector", lambda e, i=i: e.scalar_tensor_tensor(
                T["ctxc"][i], T["csc"], T["selT2"][:, i:i + 1], T["ctxc"][i],
                OP.mult, OP.add), inc="vS")
            ctx_sel_last = w.n["vS"]
        if tc_i == 0:
            for i in range(ET):
                w.dma("sync", T["wg_t"][i],
                      D["wg_bf"][i * 128:(i + 1) * 128, :], "sX2")
            wg_ld = w.n["sX2"]
        x1_ld_t = {}

        def issue_x1(i, tc_i=tc_i, c0=c0):
            sr = ("sRA", "sRB", "sXA", "sXB")[i % 4]
            if tc_i > 0:
                sgo, vo = out_store[(tc_i - 1, i)]
                w.wait("sync", sgo, vo)
            w.dma("sync", T["x2"][i],
                  D["x1"][i * 128:(i + 1) * 128, c0:c0 + TC], sr)
            x1_ld_t[i] = (sr, w.n[sr])

        for i in range(4):
            issue_x1(i)
        if tc_i == 0:
            w.wait("tensor", "sS", wts_ld)
        w.wait("tensor", "vS", ctx_sel_last)
        wo_sem = {}
        for eo in range(ET):
            wb = P(0 + (eo % 2))
            if eo >= 2:
                w.wait("tensor", "vS", wo_sem[eo - 2])
            else:
                w.wait("tensor", "vS", w.n["vS"])
            for ei in range(ET):
                w.emit("tensor", lambda e, eo=eo, ei=ei, wb=wb: e.matmul(
                    wb, T["wo_sb"][ei][:, eo * 128:eo * 128 + 128], T["ctxc"][ei],
                    start=(ei == 0), stop=(ei == ET - 1)),
                    inc="tS" if ei == ET - 1 else None)
            w.wait("vector", "tS", w.n["tS"])
            sr_, v_ = x1_ld_t[eo]
            w.wait("vector", sr_, v_)
            w.emit("vector", lambda e, eo=eo, wb=wb: e.scalar_tensor_tensor(
                T["x2"][eo], wb, T["bo_col"][:, eo:eo + 1], T["x2"][eo],
                OP.add, OP.add), inc="vS")
            wo_sem[eo] = w.n["vS"]
            if eo + 4 < ET:
                w.wait("sync", "vS", wo_sem[eo])
                issue_x1(eo + 4)
            if eo == ET - 1:
                wo_mm_done[tc_i] = w.n["tS"]
        w.wait("scalar", "vS", w.n["vS"])
        w.wait("scalar", "tS", w.n["tS"])
        st3_done = {}
        st3_cast = {}

        def emit_st3(j):
            if j == 0:
                w.wait("tensor", "vS", w.n["vS"])
            w.wait("tensor", "cS", st3_cast[j])
            w.emit("tensor", lambda e, j=j: e.matmul(
                T["P6s"], T["ones_col"], T["xb"][j],
                start=(j == 0), stop=(j == ET - 1)), inc="tS")
            w.emit("tensor", lambda e, j=j: e.matmul(
                T["P7s"], T["ones_col"], T["sqs"][j],
                start=(j == 0), stop=(j == ET - 1)), inc="tS")
            st3_done[j] = w.n["tS"]

        for i in range(ET):
            if i >= 2:
                emit_st3(i - 2)
                w.wait("scalar", "tS", st3_done[i - 2])
            w.emit("scalar", lambda e, i=i: e.copy(T["xb"][i], T["x2"][i]), inc="cS")
            w.emit("scalar", lambda e, i=i: e.activation(
                T["sqs"][i], T["x2"][i], AF.Square), inc="cS")
            st3_cast[i] = w.n["cS"]
        emit_st3(ET - 2)
        emit_st3(ET - 1)
        ln_rows(fused_double=False)
        ln_apply()

        def store_out(et, tc_i=tc_i, c0=c0):
            sgo = f"sO{et}"
            w.wait("gpsimd", "vS", w.n["vS"])
            w.dma("gpsimd", D["out"][et * 128:(et + 1) * 128, c0:c0 + TC],
                  T["x2"][et], sgo, inc_n=64)
            out_store[(tc_i, et)] = (sgo, w.n[sgo])

        if tc_i == 0:
            w.wait("tensor", "sX2", wg_ld)
        last_silu[0] = ffn_body(T["x2"], store_out,
                                entry_cs=last_silu[0], entry_gs=False)

    for eng in ENGINES:
        w.wait(eng, "gS", w.n["gS"])
        for j in range(ET):
            w.wait(eng, f"sO{j}", w.n[f"sO{j}"])


def build_nc():
    nc = bass.Bass("TRN2", target_bir_lowering=False)

    D = {}
    D["x"] = nc.dram_tensor("x", [E, TOK], F32, kind="ExternalInput").ap()
    for nm, shp in (("wq", [128, 128]), ("wk", [128, 128]), ("wv", [128, 128]),
                    ("wab", [128, 128]), ("wo", [E, E]), ("bq", [128, 1]),
                    ("bk", [128, 1]), ("bv", [128, 1]), ("ub", [128, 4]),
                    ("vb", [128, 4]), ("gamma", [1, E]), ("beta_col", [128, 8]),
                    ("gamma_col", [128, 8]),
                    ("bo_col", [128, 8]), ("selA", [128, 1]), ("selB", [128, 1]),
                    ("immc", [128, 2]), ("selT", [128, 8]),
                    ("m_lo", [128, 128]), ("m_up", [128, 128])):
        D[nm] = nc.dram_tensor(nm, shp, F32, kind="ExternalInput").ap()
    for nm, shp in (("wg_i32", [E, HID]), ("wu_i32", [E, HID]),
                    ("wd_i32", [HID, E])):
        D[nm] = nc.dram_tensor(nm, shp, I32, kind="ExternalInput").ap()
    for nm, shp in (("ident", [128, 128]), ("ones_col", [128, 1]),
                    ("ones_row", [1, 128])):
        D[nm] = nc.dram_tensor(nm, shp, BF16, kind="ExternalInput").ap()
    for nm in ("pca_lo", "psa_lo", "pca_up", "psa_up", "pe_hi"):
        t_ = nc.dram_tensor(nm, [512, S], BF16, kind="ExternalInput").ap()
        D[nm + "_h"] = [t_[hl * 128:(hl + 1) * 128, :] for hl in range(4)]

    D["out"] = nc.dram_tensor("out", [E, TOK], F32, kind="ExternalOutput").ap()

    D["x1"] = nc.dram_tensor("x1_scr", [E, TOK], F32).ap()
    D["wg_bf"] = nc.dram_tensor("wg_bf", [E, HID], BF16).ap()
    D["wu_sw"] = nc.dram_tensor("wu_sw", [HT * 128, 1024], BF16).ap()
    D["wd_cache"] = nc.dram_tensor("wd_cache", [ET * 128, HID], BF16).ap()
    D["wo_bf"] = nc.dram_tensor("wo_bf", [E, E], BF16).ap()
    D["xh_own"] = nc.dram_tensor("xh_own", [E, TOK], BF16).ap()
    D["ctx_own"] = nc.dram_tensor("ctx_own", [512, TOK], BF16).ap()
    D["cc1_in"] = nc.dram_tensor("cc1_in", [NCH * 512, TC], BF16)
    D["cc1_out"] = nc.dram_tensor("cc1_out", [NCH * 1024, TC], BF16)
    D["cc2_in"] = nc.dram_tensor("cc2_in", [4 * 128, TOK], BF16)
    D["cc2_out"] = nc.dram_tensor("cc2_out", [4 * 256, TOK], BF16)

    from contextlib import ExitStack
    with ExitStack() as ctx:
        BF_CAP = 182 * 1024
        F_CAP = 23 * 1024
        ab = ctx.enter_context(nc.sbuf_tensor("arena_bf", [128, BF_CAP // 2], BF16))
        af = ctx.enter_context(nc.sbuf_tensor("arena_f32", [128, F_CAP // 4], F32))

        T = {}
        fo = [0]
        bo = [0]

        def takef(ncols):
            off = fo[0]
            assert (off * 4) % 32 == 0
            fo[0] += ncols
            fo[0] = (fo[0] + 7) & ~7
            assert fo[0] * 4 <= F_CAP, f"f32 arena {fo[0]*4}"
            return af[:, off:off + ncols], off

        def takeb(ncols):
            off = bo[0]
            bo[0] += ncols
            bo[0] = (bo[0] + 15) & ~15
            assert bo[0] * 2 <= BF_CAP, f"bf arena {bo[0]*2}"
            return ab[:, off:off + ncols], off

        T["m_lo"], _ = takef(128)
        T["m_up"], _ = takef(128)
        smalls, _ = takef(64)
        T["bq"] = smalls[:, 0:1]
        T["bk"] = smalls[:, 1:2]
        T["bv"] = smalls[:, 2:3]
        T["selA"] = smalls[:, 3:4]
        T["selB"] = smalls[:, 4:5]
        T["negb"] = smalls[:, 5:6]
        T["ub"] = smalls[:, 8:12]
        T["vb"] = smalls[:, 12:16]
        T["quv"] = smalls[:, 16:24]
        T["beta_col"] = smalls[:, 24:32]
        T["bo_col"] = smalls[:, 32:40]
        T["mx"] = smalls[:, 40:48]
        T["Ssum"] = smalls[:, 48:52]
        T["invS"] = smalls[:, 52:56]
        T["immc"] = smalls[:, 56:58]
        selt, _ = takef(8)
        T["selT"] = selt
        selt2, _ = takef(8)
        T["selT2"] = selt2
        gcol, _ = takef(8)
        T["gamma_col"] = gcol
        rsblk, _ = takef(512)
        T["rs1"] = rsblk[0:1, :]
        rsblk2, _ = takef(512)
        T["rs2"] = rsblk2[0:1, :]
        T["c_qdeq"] = smalls[:, 56:57]
        T["c_isq"] = smalls[:, 57:58]

        T["ident"], _ = takeb(128)
        wqkv, _ = takeb(4 * 128)
        T["wq"] = wqkv[:, 0:128]
        T["wk"] = wqkv[:, 128:256]
        T["wv"] = wqkv[:, 256:384]
        T["wab"] = wqkv[:, 384:512]
        oc, _ = takeb(16)
        T["ones_col"] = oc[:, 0:1]
        orow, _ = takeb(128)
        T["ones_row"] = orow[0:1, :]
        grow, _ = takeb(1024)
        T["gamma_row"] = grow[0:1, :]
        ra, _ = takeb(TC)
        T["rowA"] = ra[0:1, :]
        rb_, _ = takeb(TC)
        T["rowB"] = rb_[0:1, :]
        rbb, _ = takeb(16)
        T["r_b"] = rbb[0:1, 0:1]
        mxbb, _ = takeb(16)
        T["mxb"] = mxbb[:, 0:2]

        f_mark = fo[0]
        T["xf0"], xf0_off = takef(TC)
        T["xf1"], _ = takef(TC)
        T["xf2"], _ = takef(TC)
        T["xf3"], _ = takef(TC)
        T["xf01_row"] = af[0:1, xf0_off:xf0_off + 1024]
        T["xf01_blk"] = af[:, xf0_off:xf0_off + 1024]
        p1_f = fo[0]
        fo[0] = f_mark
        T["strip"], _ = takef(256)
        p2_f = fo[0]
        fo[0] = f_mark
        T["x2"] = [takef(TC)[0] for _ in range(ET)]
        p34_f = fo[0]
        fo[0] = max(p1_f, p2_f, p34_f)
        assert fo[0] * 4 <= F_CAP, f"f32 total {fo[0]*4}"

        b_mark = bo[0]
        T["wg_t"] = [takeb(HID)[0] for _ in range(ET)]
        T["wubuf"] = [takeb(1024)[0] for _ in range(2)]
        T["wdbuf"] = [takeb(HID)[0] for _ in range(2)]
        T["wo_sb"] = [takeb(1024)[0] for _ in range(ET)]
        T["xb0"] = [takeb(TC)[0] for _ in range(ET)]
        T["xb1"] = [takeb(TC)[0] for _ in range(ET)]
        h_t = []
        h_offs = []
        for _ in range(HT):
            t_, o_ = takeb(TC)
            h_t.append(t_)
            h_offs.append(o_)
        T["h_t"] = h_t
        T["silu"] = [takeb(TC)[0] for _ in range(2)]
        T["sqs"] = [takeb(TC)[0] for _ in range(8)]
        T["sqs2"] = [takeb(TC)[0] for _ in range(2)]
        T["cb"] = [takeb(TC)[0] for _ in range(4)]
        T["ctxc"] = [takeb(TC)[0] for _ in range(ET)]
        T["csc"], _ = takeb(TC)
        p14_b = bo[0]
        h0 = h_offs[0]
        T["istage"] = [
            ab[:, h0:h0 + 3072].bitcast(I32),
            ab[:, h0 + 3072:h0 + 6144].bitcast(I32),
        ]
        T["stagebf"] = [ab[:, h0 + 6144:h0 + 7168], ab[:, h0 + 7168:h0 + 8192]]
        T["wuconv"] = ab[:, h0 + 8192:h0 + 11264]
        bo[0] = b_mark
        T["xhh"] = [takeb(S)[0] for _ in range(4)]

        T["xsc"], _ = takeb(S)
        T["Qu"], _ = takeb(S)
        T["Qv"], _ = takeb(S)
        T["ABswap"] = T["Qv"]
        T["Kb"], _ = takeb(S)
        T["Vt"], _ = takeb(KC * 129)
        T["AB"], _ = takeb(S)
        T["UVlo"], _ = takeb(S)
        T["UVup"], _ = takeb(S)
        T["T1"], _ = takeb(S)
        T["T2"], _ = takeb(S)
        T["pca"], _ = takeb(S)
        T["psa"], _ = takeb(S)
        T["pe_hi"], _ = takeb(S)
        T["Pt"], _ = takeb(4 * 512)
        ctxtm_, _ = takeb(4 * 128)
        T["ctx_tm4"] = [ctxtm_[:, j * 128:(j + 1) * 128] for j in range(4)]
        T["ctxf"] = [takeb(S)[0] for _ in range(4)]
        p2_b = bo[0]
        bo[0] = max(p14_b, p2_b)
        assert bo[0] * 2 <= BF_CAP, f"bf16 total {bo[0]*2}"

        psA = ctx.enter_context(nc.psum_tensor("psA", [128, 1024], F32))
        psB = ctx.enter_context(nc.psum_tensor("psB", [128, 1024], F32))
        pb4 = ctx.enter_context(nc.psum_tensor("pb4", [128, 512], F32))
        pb5 = ctx.enter_context(nc.psum_tensor("pb5", [128, 512], F32))
        pb6 = ctx.enter_context(nc.psum_tensor("pb6", [128, 512], F32))
        pb7 = ctx.enter_context(nc.psum_tensor("pb7", [128, 512], F32))
        PB = [psA[:, 0:512], psA[:, 512:1024], psB[:, 0:512], psB[:, 512:1024],
              pb4[:, :], pb5[:, :], pb6[:, :], pb7[:, :]]
        T["P6s"] = pb6[0:1, 0:TC]
        T["P7s"] = pb7[0:1, 0:TC]

        sems = {}
        for s_ in ("sS", "gS", "tS", "vS", "cS", "bar", "sXA", "sXB", "sRA",
                   "sRB", "sWA", "sWB", "sT", "sH", "sC", "sX2", "sGA", "sGB",
                   "cc", "sSW", "sHA", "sHB", "sUA", "sUB",
                   "sO0", "sO1", "sO2", "sO3", "sO4", "sO5", "sO6", "sO7"):
            sems[s_] = ctx.enter_context(nc.semaphore(s_))

        block = ctx.enter_context(nc.Block())

        @block.sync
        def _(sync):
            _program(W("sync", sync, sems), T, D, PB)

        @block.tensor
        def _(tensor):
            _program(W("tensor", tensor, sems), T, D, PB)

        @block.vector
        def _(vector):
            _program(W("vector", vector, sems), T, D, PB)

        @block.scalar
        def _(scalar):
            _program(W("scalar", scalar, sems), T, D, PB)

        @block.gpsimd
        def _(gpsimd):
            _program(W("gpsimd", gpsimd, sems), T, D, PB)

    return nc


_NC = None


def kernel(L0_attn_inputs, L0_attn_ln_gamma, L0_attn_ln_beta, L0_attn_Wq,
           L0_attn_bq, L0_attn_Wk, L0_attn_bk, L0_attn_Wv, L0_attn_bv,
           L0_attn_Wp, L0_attn_u_bias, L0_attn_v_bias, L0_attn_Wo, L0_attn_bo,
           L0_ffn_w_gate, L0_ffn_w_up, L0_ffn_w_down):
    global _NC
    if _NC is None:
        _NC = build_nc()
    nc = _NC
    C = _consts()

    x = np.asarray(L0_attn_inputs, dtype=np.float32)
    Wp = np.asarray(L0_attn_Wp, dtype=np.float32)
    wab = np.ascontiguousarray(np.concatenate([Wp[0::2, :], Wp[1::2, :]], axis=0).T)
    shared = dict(
        wq=np.ascontiguousarray(L0_attn_Wq, dtype=np.float32),
        wk=np.ascontiguousarray(L0_attn_Wk, dtype=np.float32),
        wv=np.ascontiguousarray(L0_attn_Wv, dtype=np.float32),
        wab=wab,
        wo=np.ascontiguousarray(L0_attn_Wo, dtype=np.float32),
        bq=np.ascontiguousarray(np.asarray(L0_attn_bq, np.float32)[:, None]),
        bk=np.ascontiguousarray(np.asarray(L0_attn_bk, np.float32)[:, None]),
        bv=np.ascontiguousarray(np.asarray(L0_attn_bv, np.float32)[:, None]),
        gamma=np.ascontiguousarray(np.asarray(L0_attn_ln_gamma, np.float32)[None, :]),
        beta_col=np.ascontiguousarray(
            np.asarray(L0_attn_ln_beta, np.float32).reshape(8, 128).T),
        gamma_col=np.ascontiguousarray(
            np.asarray(L0_attn_ln_gamma, np.float32).reshape(8, 128).T),
        bo_col=np.ascontiguousarray(
            np.asarray(L0_attn_bo, np.float32).reshape(8, 128).T),
        wg_i32=np.ascontiguousarray(np.asarray(L0_ffn_w_gate, np.int32).T),
        wu_i32=np.ascontiguousarray(np.asarray(L0_ffn_w_up, np.int32).T),
        wd_i32=np.ascontiguousarray(np.asarray(L0_ffn_w_down, np.int32).T),
        ident=C["ident"], m_lo=C["m_lo"], m_up=C["m_up"],
        ones_col=C["ones_col"], ones_row=C["ones_row"],
        immc=np.ascontiguousarray(
            np.broadcast_to(np.array([QDEQ, ISQ], np.float32), (128, 2))),
    )
    ub_t = np.asarray(L0_attn_u_bias, np.float32).T
    vb_t = np.asarray(L0_attn_v_bias, np.float32).T

    in_maps = []
    for c in range(8):
        b = c // 2
        half = c % 2
        t0 = half * TOK
        hsl = slice(half * 512, half * 512 + 512)
        m = dict(shared)
        m["x"] = np.ascontiguousarray(x[b, t0:t0 + TOK, :].T)
        m["ub"] = np.ascontiguousarray(ub_t[:, 4 * half:4 * half + 4])
        m["vb"] = np.ascontiguousarray(vb_t[:, 4 * half:4 * half + 4])
        m["selA"] = np.full((128, 1), 1.0 if half == 0 else 0.0, np.float32)
        m["selB"] = np.full((128, 1), 0.0 if half == 0 else 1.0, np.float32)
        st_ = np.zeros((128, 8), np.float32)
        for i_ in range(8):
            st_[:, i_] = 1.0 if ((i_ < 4) == (half == 0)) else 0.0
        m["selT"] = st_
        for nm in ("pca_lo", "psa_lo", "pca_up", "psa_up", "pe_hi"):
            m[nm] = np.ascontiguousarray(C[nm][hsl])
        in_maps.append(m)

    res = run_bass_kernel_spmd(nc, in_maps, core_ids=list(range(8)))

    out = np.empty((B, S, E), dtype=np.float32)
    for c in range(8):
        b = c // 2
        t0 = (c % 2) * TOK
        out[b, t0:t0 + TOK, :] = res.results[c]["out"].T
    return out


# version 17 (final)
# speedup vs baseline: 1.0154x; 1.0154x over previous
import math
import os
import numpy as np
import ml_dtypes

import concourse.bass as bass
import concourse.mybir as mybir
from concourse.bass_utils import run_bass_kernel_spmd
from concourse.alu_op_type import AluOpType

F32 = mybir.dt.float32
BF16 = mybir.dt.bfloat16
I32 = mybir.dt.int32
AF = mybir.ActivationFunctionType
AX = mybir.AxisListType
OP = AluOpType

B, S, E, H, DH = 4, 4096, 1024, 8, 128
NF = 64
TOK = 2048
TC = 512
NCH = TOK // TC
ET = E // 128
HID = 3072
HT = HID // 128
KC = S // 128
QT = S // 512
EPS = 1e-5
QDEQ = 1.0 / 127.0
ISQ = 1.0 / 32.0

ENGINES = ("sync", "tensor", "vector", "scalar", "gpsimd")


def _make_consts():
    div = np.exp(np.arange(0, E, 2, dtype=np.float64) * (-math.log(10000.0) / E))
    pca_lo = np.zeros((H * 128, S), np.float64)
    psa_lo = np.zeros((H * 128, S), np.float64)
    pca_up = np.zeros((H * 128, S), np.float64)
    psa_up = np.zeros((H * 128, S), np.float64)
    pe_hi = np.zeros((H * 128, S), np.float64)
    q_idx = np.arange(S, dtype=np.float64)
    for h in range(H):
        w = div[h * NF:(h + 1) * NF]
        a_lo = q_idx[None, :] * w[:, None]
        a_up = (q_idx[None, :] + S + 1) * w[:, None]
        a_hi = (q_idx[None, :] + S - 1) * w[:, None]
        for half in (0, 64):
            pca_lo[h * 128 + half:h * 128 + half + 64] = np.cos(a_lo)
            psa_lo[h * 128 + half:h * 128 + half + 64] = np.sin(a_lo)
            pca_up[h * 128 + half:h * 128 + half + 64] = np.cos(a_up)
            psa_up[h * 128 + half:h * 128 + half + 64] = np.sin(a_up)
        pe_hi[h * 128:h * 128 + 64] = np.sin(a_hi)
        pe_hi[h * 128 + 64:h * 128 + 128] = np.cos(a_hi)
    ki = np.arange(128)[:, None]
    jj = np.arange(128)[None, :]
    bf = ml_dtypes.bfloat16
    return dict(
        pca_lo=pca_lo.astype(bf), psa_lo=psa_lo.astype(bf),
        pca_up=pca_up.astype(bf), psa_up=psa_up.astype(bf), pe_hi=pe_hi.astype(bf),
        m_lo=(ki < jj).astype(np.float32), m_up=(ki > jj).astype(np.float32),
        ident=np.eye(128, dtype=np.float32).astype(bf),
        ones_col=np.ones((128, 1), np.float32).astype(bf),
        ones_row=np.ones((1, 128), np.float32).astype(bf),
    )


_CONSTS = None


def _consts():
    global _CONSTS
    if _CONSTS is None:
        _CONSTS = _make_consts()
    return _CONSTS


class W:

    def __init__(self, target, eng, sems):
        self.t = target
        self.e = eng
        self.sems = sems
        self.n = {k: 0 for k in sems}

    def wait(self, engine, sem, val=None):
        v = self.n[sem] if val is None else val
        if v <= 0:
            return
        if self.t == engine:
            self.e.wait_ge(self.sems[sem], v)

    def emit(self, engine, fn, inc=None, inc_n=1):
        if engine == "vector" or engine == "scalar":
            sem = "vS" if engine == "vector" else "cS"
            assert inc in (None, sem), (engine, inc)
            if self.t == engine:
                if self.n[sem] > 0:
                    self.e.wait_ge(self.sems[sem], self.n[sem])
                inst = fn(self.e)
                inst.then_inc(self.sems[sem], 1)
            self.n[sem] += 1
            return
        if self.t == engine:
            inst = fn(self.e)
            if inc is not None:
                inst.then_inc(self.sems[inc], inc_n)
        if inc is not None:
            self.n[inc] += inc_n

    def dma(self, engine, out, in_, sem, inc_n=16):
        self.emit(engine, lambda e: e.dma_start(out=out, in_=in_),
                  inc=sem, inc_n=inc_n)

    DATA_SEMS = ("tS", "vS", "cS", "sS", "gS", "sT", "sH", "sC", "sX2", "sXA",
                 "sXB", "sRA", "sRB", "sWA", "sWB", "sGA", "sGB", "cc", "sSW",
                 "sHA", "sHB", "sUA", "sUB",
                 "sO0", "sO1", "sO2", "sO3", "sO4", "sO5", "sO6", "sO7")

    def barrier(self):
        for eng in ENGINES:
            for sm in self.DATA_SEMS:
                self.wait(eng, sm)


PHASES = os.environ.get("MK_PHASES", "all")
P2LVL = int(os.environ.get("MK_P2", "99"))
HSLVL = int(os.environ.get("MK_HS", "99"))
NOSTRIP = os.environ.get("MK_NOSTRIP", "0") == "1"


def _program(w, T, D, PB):
    def P(i):
        return PB[i]

    def finish_early():
        w.wait("gpsimd", "vS", w.n["vS"])
        w.wait("gpsimd", "tS", w.n["tS"])
        w.wait("gpsimd", "cS", w.n["cS"])
        w.wait("gpsimd", "sS", w.n["sS"])
        w.wait("gpsimd", "gS", w.n["gS"])
        for et in range(ET):
            for tc in range(NCH):
                w.dma("gpsimd", D["out"][et * 128:(et + 1) * 128,
                                         tc * TC:(tc + 1) * TC], T["xf0"], "gS")
        for eng in ENGINES:
            w.wait(eng, "gS", w.n["gS"])

    for nm in ("ident", "m_lo", "m_up", "ones_col", "ones_row", "bq", "bk",
               "bv", "ub", "vb", "beta_col", "bo_col", "gamma_col", "selA",
               "selB", "immc", "selT"):
        w.dma("sync", T[nm], D[nm], "sS")
    for src in ("wq", "wk", "wv", "wab"):
        w.wait("sync", "vS", w.n["vS"])
        w.dma("sync", T["xf0"][:, 0:128], D[src], "sS")
        w.wait("vector", "sS", w.n["sS"])
        w.emit("vector", lambda e, src=src: e.tensor_copy(
            T[src], T["xf0"][:, 0:128]), inc="vS")
    w.wait("sync", "vS", w.n["vS"])
    w.dma("sync", T["xf01_row"], D["gamma"], "sS")
    w.wait("vector", "sS", w.n["sS"])
    w.emit("vector", lambda e: e.tensor_copy(T["gamma_row"], T["xf01_row"]), inc="vS")
    w.emit("vector", lambda e: e.tensor_scalar_add(T["quv"][:, 0:4], T["ub"], T["bq"]))
    w.emit("vector", lambda e: e.tensor_scalar_add(T["quv"][:, 4:8], T["vb"], T["bq"]),
           inc="vS")
    w.emit("vector", lambda e: e.tensor_scalar(
        T["selT2"], T["selT"], -1.0, 1.0, OP.mult, OP.add), inc="vS")
    w.wait("sync", "vS", w.n["vS"])

    stbf_free = [0, 0]
    for i in range(ET):
        w.wait("sync", "vS", w.n["vS"])
        w.dma("sync", T["xf01_blk"], D["wo"][i * 128:(i + 1) * 128, :], "sS")
        w.wait("vector", "gS", w.n["gS"])
        w.wait("vector", "sS", w.n["sS"])
        w.emit("vector", lambda e, i=i: e.tensor_copy(
            T["stagebf"][i % 2], T["xf01_blk"]), inc="vS")
        w.wait("gpsimd", "vS", w.n["vS"])
        w.dma("gpsimd", D["wo_bf"][i * 128:(i + 1) * 128, :], T["stagebf"][i % 2],
              "gS")
        stbf_free[i % 2] = w.n["gS"]
    xf_free = w.n["vS"]

    cast_after = []

    def stage_guard(k):
        if k >= 2:
            w.wait("sync", "vS", cast_after[k - 2])

    kstage = [0]

    def conv_big(dst_tiles, dram, ntiles, cols):
        hc = cols // 2
        for i in range(ntiles):
            for hf in range(2):
                k = kstage[0]
                buf = T["istage"][k % 2]
                sw = "sWA" if (k % 2 == 0) else "sWB"
                stage_guard(k)
                w.dma("sync", buf, dram[i * 128:(i + 1) * 128,
                                        hf * hc:(hf + 1) * hc], sw)
                w.wait("vector", sw, w.n[sw])
                w.emit("vector", lambda e, i=i, hf=hf, buf=buf, hc=hc: e.tensor_copy(
                    dst_tiles[i][:, hf * hc:(hf + 1) * hc], buf), inc="vS")
                cast_after.append(w.n["vS"])
                kstage[0] += 1

    conv_big(T["wg_t"], D["wg_i32"], ET, HID)
    w.wait("gpsimd", "vS", w.n["vS"])
    for i in range(ET):
        w.dma("gpsimd", D["wg_bf"][i * 128:(i + 1) * 128, :], T["wg_t"][i], "gS")
    wu_view = D["wu_sw"].rearrange("(ht p) c -> p ht c", ht=HT)
    for i in range(ET):
        for hf in range(2):
            k = kstage[0]
            buf = T["istage"][k % 2]
            sw = "sWA" if (k % 2 == 0) else "sWB"
            stage_guard(k)
            w.dma("sync", buf, D["wu_i32"][i * 128:(i + 1) * 128,
                                           hf * 1536:(hf + 1) * 1536], sw)
            w.wait("vector", "gS", w.n["gS"])
            w.wait("vector", sw, w.n[sw])
            w.emit("vector", lambda e, buf=buf, hf=hf: e.tensor_copy(
                T["wuconv"][:, hf * 1536:(hf + 1) * 1536], buf), inc="vS")
            cast_after.append(w.n["vS"])
            kstage[0] += 1
        w.wait("gpsimd", "vS", w.n["vS"])
        src = T["wuconv"].rearrange("p (ht j) -> p ht j", ht=HT)
        w.dma("gpsimd", wu_view[:, :, i * 128:(i + 1) * 128], src, "gS")
    wd_view = D["wd_cache"].rearrange("(et p) c -> p et c", et=8)
    for i in range(HT):
        k = kstage[0]
        buf = T["istage"][k % 2]
        sw = "sWA" if (k % 2 == 0) else "sWB"
        stage_guard(k)
        w.dma("sync", buf[:, 0:1024], D["wd_i32"][i * 128:(i + 1) * 128, :], sw)
        w.wait("vector", "gS", w.n["gS"])
        w.wait("vector", sw, w.n[sw])
        w.emit("vector", lambda e, i=i, buf=buf: e.tensor_copy(
            T["stagebf"][i % 2], buf[:, 0:1024]), inc="vS")
        cast_after.append(w.n["vS"])
        kstage[0] += 1
        w.wait("gpsimd", "vS", w.n["vS"])
        src = T["stagebf"][i % 2].rearrange("p (et j) -> p et j", et=8)
        w.dma("gpsimd", wd_view[:, :, i * 128:(i + 1) * 128], src, "gS")
        stbf_free[i % 2] = w.n["gS"]
    wd_cache_ready = w.n["gS"]
    wu_sw_ready = wd_cache_ready
    conv_all_done = w.n["vS"]
    p0_gs = w.n["gS"]

    def ln_rows(fused_double=False, sx=None, sxx=None, rbank=4):
        sx = sx if sx is not None else T["P6s"]
        sxx = sxx if sxx is not None else T["P7s"]
        r4 = PB[rbank][0:1, 0:TC]
        rs1, rs2 = T["rs1"], T["rs2"]
        w.wait("vector", "tS", w.n["tS"])
        w.emit("vector", lambda e, rs1=rs1, sx=sx: e.tensor_scalar_mul(
            rs1, sx, 1.0 / E))
        w.emit("vector", lambda e, rs2=rs2, sxx=sxx: e.tensor_scalar_mul(
            rs2, sxx, 1.0 / E))
        w.emit("vector", lambda e, r4=r4, rs1=rs1: e.scalar_tensor_tensor(
            r4, rs1, -1.0, rs1, OP.mult, OP.mult))
        w.emit("vector", lambda e, r4=r4, rs2=rs2: e.tensor_tensor(
            r4, r4, rs2, OP.add))
        sc = (1.0 + EPS) if fused_double else 1.0
        bi = (EPS * EPS) if fused_double else EPS
        w.emit("vector", lambda e, r4=r4, sc=sc, bi=bi: e.tensor_scalar(
            r4, r4, sc, bi, OP.mult, OP.add), inc="vS")
        w.wait("scalar", "vS", w.n["vS"])
        w.emit("scalar", lambda e, r4=r4, rs2=rs2: e.activation(
            rs2, r4, AF.Sqrt), inc="cS")
        w.wait("vector", "cS", w.n["cS"])
        w.emit("vector", lambda e, r4=r4, rs2=rs2: e.reciprocal(r4, rs2))
        w.emit("vector", lambda e, r4=r4: e.tensor_copy(T["rowA"], r4))
        w.emit("vector", lambda e, r4=r4, rs1=rs1: e.scalar_tensor_tensor(
            T["rowB"], rs1, -1.0, r4, OP.mult, OP.mult), inc="vS")

    def ln_apply(xbset=None, ba=6, bb=7):
        xbset = xbset if xbset is not None else T["xb"]
        w.wait("tensor", "vS", w.n["vS"])
        w.emit("tensor", lambda e, ba=ba: e.matmul(
            P(ba), T["ones_row"], T["rowA"], start=True, stop=True), inc="tS")
        w.emit("tensor", lambda e, bb=bb: e.matmul(
            P(bb), T["ones_row"], T["rowB"], start=True, stop=True), inc="tS")
        w.wait("vector", "tS", w.n["tS"])
        for i in range(ET):
            w.emit("vector", lambda e, i=i, xbset=xbset, ba=ba: e.tensor_tensor(
                xbset[i], xbset[i], P(ba), OP.mult))
            w.emit("vector", lambda e, i=i, xbset=xbset, bb=bb: e.tensor_tensor(
                xbset[i], xbset[i], P(bb), OP.add),
                inc="vS" if i == ET - 1 else None)

    prev_free = {"wu": (0, 0), "wd": (0, 0)}

    def ffn_body(res_tiles, store_fn, res_prewait=None, mid_hook=None,
                 ln_wait=None, top_hook=None, entry_cs=None, entry_gs=True,
                 ht_hook=None):
        ln_apply_done = w.n["vS"] if ln_wait is None else ln_wait
        if top_hook is not None:
            top_hook()
        silu_sem = {}
        h_sem = {}
        w.wait("tensor", "vS", ln_apply_done)
        w.wait("tensor", "cS", w.n["cS"] if entry_cs is None else entry_cs)
        if entry_gs:
            w.wait("vector", "gS", w.n["gS"])
        up_mm_hist = {}
        wu_ld_v = {}

        def issue_wu(ht):
            wub = T["wubuf"][ht % 2]
            su = "sUA" if (ht % 2 == 0) else "sUB"
            if ht >= 2:
                w.wait("sync", "tS", up_mm_hist[ht - 2])
            elif prev_free["wu"] == (0, 0):
                w.wait("sync", "gS", wu_sw_ready)
            else:
                w.wait("sync", "tS", prev_free["wu"][ht])
            w.dma("sync", wub, D["wu_sw"][ht * 128:(ht + 1) * 128, :], su)
            wu_ld_v[ht] = w.n[su]

        issue_wu(0)
        issue_wu(1)
        for ht in range(HT):
            gb = P(0 + (ht % 2))
            ub_ = P(2 + (ht % 2))
            wub = T["wubuf"][ht % 2]
            su = "sUA" if (ht % 2 == 0) else "sUB"
            if ht >= 2:
                w.wait("tensor", "cS", silu_sem[ht - 2])
                w.wait("tensor", "vS", h_sem[ht - 2])
            for ec in range(ET):
                w.emit("tensor", lambda e, ht=ht, ec=ec, gb=gb: e.matmul(
                    gb, T["wg_t"][ec][:, ht * 128:(ht + 1) * 128], T["xb"][ec],
                    start=(ec == 0), stop=(ec == ET - 1)),
                    inc="tS" if ec == ET - 1 else None)
            gate_mm = w.n["tS"]
            w.wait("tensor", su, wu_ld_v[ht])
            for ec in range(ET):
                w.emit("tensor", lambda e, ht=ht, ec=ec, ub_=ub_, wub=wub: e.matmul(
                    ub_, wub[:, ec * 128:(ec + 1) * 128], T["xb"][ec],
                    start=(ec == 0), stop=(ec == ET - 1)),
                    inc="tS" if ec == ET - 1 else None)
            up_mm = w.n["tS"]
            up_mm_hist[ht] = up_mm
            if ht + 2 < HT:
                issue_wu(ht + 2)
            w.wait("scalar", "tS", gate_mm)
            w.emit("scalar", lambda e, ht=ht, gb=gb: e.activation(
                T["silu"][ht % 2], gb, AF.Silu, scale=T["c_qdeq"]), inc="cS")
            silu_sem[ht] = w.n["cS"]
            w.wait("vector", "tS", up_mm)
            w.wait("vector", "cS", silu_sem[ht])
            w.emit("vector", lambda e, ht=ht, ub_=ub_: e.scalar_tensor_tensor(
                T["h_t"][ht], ub_, QDEQ, T["silu"][ht % 2], OP.mult, OP.mult),
                inc="vS")
            h_sem[ht] = w.n["vS"]
            if ht_hook is not None:
                ht_hook(ht)
        if mid_hook is not None:
            mid_hook(silu_sem[HT - 1], h_sem[HT - 1])
        dn_sem = {}
        res_sem = {}
        wd_ld_v = {}

        def issue_wd(et):
            buf = T["wdbuf"][et % 2]
            sw = "sWA" if (et % 2 == 0) else "sWB"
            if et >= 2:
                w.wait("sync", "tS", dn_sem[et - 2])
            elif prev_free["wd"] == (0, 0):
                w.wait("sync", "gS", wd_cache_ready)
            else:
                w.wait("sync", "tS", prev_free["wd"][et])
            w.dma("sync", buf, D["wd_cache"][et * 128:(et + 1) * 128, :], sw)
            wd_ld_v[et] = w.n[sw]

        issue_wd(0)
        issue_wd(1)
        for et in range(ET):
            db = P(4 + (et % 2))
            buf = T["wdbuf"][et % 2]
            sw = "sWA" if (et % 2 == 0) else "sWB"
            if et >= 2:
                w.wait("tensor", "vS", res_sem[et - 2])
            w.wait("tensor", sw, wd_ld_v[et])
            w.wait("tensor", "vS", h_sem[HT - 1])
            for ht in range(HT):
                w.emit("tensor", lambda e, ht=ht, db=db, buf=buf: e.matmul(
                    db, buf[:, ht * 128:(ht + 1) * 128], T["h_t"][ht],
                    start=(ht == 0), stop=(ht == HT - 1)),
                    inc="tS" if ht == HT - 1 else None)
            dn_sem[et] = w.n["tS"]
            if et + 2 < ET:
                issue_wd(et + 2)
            if res_prewait is not None:
                res_prewait(et)
            w.wait("vector", "tS", dn_sem[et])
            w.emit("vector", lambda e, et=et, db=db: e.scalar_tensor_tensor(
                res_tiles[et], db, 0.5 * QDEQ, res_tiles[et], OP.mult, OP.add),
                inc="vS")
            res_sem[et] = w.n["vS"]
            store_fn(et)
        prev_free["wu"] = (up_mm_hist[HT - 2], up_mm_hist[HT - 1])
        prev_free["wd"] = (dn_sem[ET - 2], dn_sem[ET - 1])
        return silu_sem[HT - 1]

    if PHASES == "p0":
        finish_early()
        return

    x1_cast_done = {}
    x1_store_done = {}
    prev_sq = {}
    prev_stats = {}
    st2_prev = {}
    ln_ready = {}
    xh_vec_done = {}
    last_silu = [None]

    ln1_sq = {}
    ln1_mm_done = {}

    ln1_ld = {}

    def ln1_load_issue(tc_i, i):
        c0 = tc_i * TC
        buf = T["xf0"] if (i % 2 == 0) else T["xf1"]
        sx = "sXA" if (i % 2 == 0) else "sXB"
        if i >= 2:
            w.wait("sync", "cS", ln1_sq[(tc_i, i - 2)])
        elif tc_i > 0:
            w.wait("sync", "cS", prev_sq[tc_i - 1])
        elif tc_i == 0:
            w.wait("sync", "vS", xf_free)
        w.dma("sync", buf, D["x"][i * 128:(i + 1) * 128, c0:c0 + TC], sx)
        ln1_ld[(tc_i, i)] = (sx, w.n[sx])

    def ln1_pair(tc_i, i):
        buf = T["xf0"] if (i % 2 == 0) else T["xf1"]
        xbset = T["xb0"] if (tc_i % 2 == 0) else T["xb1"]
        if i == 0:
            sHp = "sHA" if (tc_i % 2 == 0) else "sHB"
            if tc_i >= 2:
                w.wait("scalar", sHp, w.n[sHp])
                w.wait("scalar", "tS", prev_stats[tc_i - 2])
            if tc_i >= 1:
                w.wait("scalar", "tS", ln1_mm_done[tc_i - 1])
        sx, v = ln1_ld[(tc_i, i)]
        w.wait("scalar", sx, v)
        w.emit("scalar", lambda e, i=i, buf=buf, xbset=xbset: e.copy(
            xbset[i], buf), inc="cS")
        w.emit("scalar", lambda e, i=i, buf=buf: e.activation(
            T["sqs"][i], buf, AF.Square), inc="cS")
        ln1_sq[(tc_i, i)] = w.n["cS"]
        if i == ET - 1:
            prev_sq[tc_i] = w.n["cS"]

    def emit_ln1_loads(tc_i):
        for i in range(ET):
            ln1_load_issue(tc_i, i)
            ln1_pair(tc_i, i)

    def emit_ln1_mms(tc_i, bSx, bSxx, brr, pre_waits=()):
        xbset = T["xb0"] if (tc_i % 2 == 0) else T["xb1"]
        for (eng_, sem_, val_) in pre_waits:
            w.wait(eng_, sem_, val_)
        if tc_i > 0:
            w.wait("tensor", "vS", w.n["vS"])
        for i in range(ET):
            w.wait("tensor", "cS", ln1_sq[(tc_i, i)])
            w.emit("tensor", lambda e, i=i, bSx=bSx, xbset=xbset: e.matmul(
                PB[bSx][0:1, 0:TC], T["ones_col"], xbset[i],
                start=(i == 0), stop=(i == ET - 1)), inc="tS")
            w.emit("tensor", lambda e, i=i, bSxx=bSxx: e.matmul(
                PB[bSxx][0:1, 0:TC], T["ones_col"], T["sqs"][i],
                start=(i == 0), stop=(i == ET - 1)), inc="tS")
        ln1_mm_done[tc_i] = w.n["tS"]
        ln_rows(fused_double=False, sx=PB[bSx][0:1, 0:TC], sxx=PB[bSxx][0:1, 0:TC],
                rbank=brr)
        if tc_i == 0:
            w.wait("tensor", "vS", conv_all_done)
        ln_apply(xbset=xbset, ba=bSx, bb=bSxx)
        ln_ready[tc_i] = w.n["vS"]

    def emit_ln1(tc_i, bSx, bSxx, brr, pre_waits=()):
        emit_ln1_loads(tc_i)
        emit_ln1_mms(tc_i, bSx, bSxx, brr, pre_waits)

    for tc_i in range(NCH):
        c0 = tc_i * TC
        T["xb"] = T["xb0"] if (tc_i % 2 == 0) else T["xb1"]
        sHp = "sHA" if (tc_i % 2 == 0) else "sHB"
        if tc_i == 0:
            emit_ln1(0, 6, 7, 4)

        xres = {et: (T["xf2"] if (et % 2 == 0) else T["xf3"]) for et in range(ET)}
        resld = {}

        def issue_reload(et, tc_i=tc_i, c0=c0):
            sr = "sRA" if (et % 2 == 0) else "sRB"
            sg = "sGA" if (et % 2 == 0) else "sGB"
            if et >= 2:
                w.wait("sync", "cS", x1_cast_done[(tc_i, et - 2)])
                w.wait("sync", sg, w.n[sg])
            elif tc_i > 0:
                w.wait("sync", "cS", x1_cast_done[(tc_i - 1, ET - 2 + et)])
                w.wait("sync", sg, w.n[sg])
            w.dma("sync", xres[et], D["x"][et * 128:(et + 1) * 128, c0:c0 + TC], sr)
            resld[et] = (sr, w.n[sr])

        issue_reload(0)
        issue_reload(1)

        def res_prewait(et, resld=resld):
            sr, v = resld[et]
            w.wait("vector", sr, v)

        st2_done = {}

        def emit_st2(j, tc_i=tc_i):
            if j == 0:
                w.wait("tensor", "vS", xh_vec_done.get(tc_i - 1, 0))
            w.wait("tensor", "cS", x1_cast_done[(tc_i, j)])
            w.emit("tensor", lambda e, j=j: e.matmul(
                T["P6s"], T["ones_col"], T["xb"][j],
                start=(j == 0), stop=(j == ET - 1)), inc="tS")
            w.emit("tensor", lambda e, j=j: e.matmul(
                T["P7s"], T["ones_col"], T["sqs2"][j % 2],
                start=(j == 0), stop=(j == ET - 1)), inc="tS")
            st2_done[j] = w.n["tS"]
            st2_prev[(tc_i, j)] = w.n["tS"]
            if j == ET - 1:
                prev_stats[tc_i] = w.n["tS"]

        def store_x1(et, tc_i=tc_i, c0=c0, xres=xres):
            sg = "sGA" if (et % 2 == 0) else "sGB"
            w.wait("gpsimd", "vS", w.n["vS"])
            w.dma("gpsimd", D["x1"][et * 128:(et + 1) * 128, c0:c0 + TC],
                  xres[et], sg)
            x1_store_done[(tc_i, et)] = w.n[sg]
            if et >= 2:
                emit_st2(et - 2)
            w.wait("scalar", "vS", w.n["vS"])
            if et >= 2:
                w.wait("scalar", "tS", st2_done[et - 2])
            elif tc_i > 0:
                w.wait("scalar", "tS", st2_prev[(tc_i - 1, ET - 2 + et)])
            w.emit("scalar", lambda e, et=et, xres=xres: e.copy(
                T["xb"][et], xres[et]), inc="cS")
            w.emit("scalar", lambda e, et=et, xres=xres: e.activation(
                T["sqs2"][et % 2], xres[et], AF.Square), inc="cS")
            x1_cast_done[(tc_i, et)] = w.n["cS"]
            if et == ET - 1:
                emit_st2(ET - 2)
                emit_st2(ET - 1)
            if et + 2 < ET:
                issue_reload(et + 2)

        def p1_top(tc_i=tc_i):
            if tc_i + 1 < NCH:
                ln1_load_issue(tc_i + 1, 0)
                ln1_load_issue(tc_i + 1, 1)

        def p1_ht(ht, tc_i=tc_i):
            if tc_i + 1 < NCH and ht % 3 == 0 and ht // 3 < ET:
                i = ht // 3
                ln1_pair(tc_i + 1, i)
                if i + 2 < ET:
                    ln1_load_issue(tc_i + 1, i + 2)

        def p1_hook(silu_last, h_last, tc_i=tc_i):
            if tc_i + 1 < NCH:
                emit_ln1_mms(tc_i + 1, 0, 1, 2,
                             pre_waits=(("tensor", "cS", silu_last),
                                        ("tensor", "vS", h_last)))

        last_silu[0] = ffn_body(
            xres, store_x1, res_prewait, mid_hook=p1_hook,
            ln_wait=ln_ready[tc_i], top_hook=p1_top, ht_hook=p1_ht,
            entry_cs=last_silu[0], entry_gs=(tc_i == 0))

        ln_rows(fused_double=True)
        w.wait("tensor", "vS", w.n["vS"])
        w.emit("tensor", lambda e: e.matmul(
            P(6), T["ones_row"], T["rowA"], start=True, stop=True), inc="tS")
        w.emit("tensor", lambda e: e.matmul(
            P(7), T["ones_row"], T["rowB"], start=True, stop=True), inc="tS")
        w.wait("vector", "tS", w.n["tS"])
        xh_cs = {}
        for i in range(ET):
            w.emit("vector", lambda e, i=i: e.tensor_tensor(
                T["xb"][i], T["xb"][i], P(6), OP.mult))
            w.emit("vector", lambda e, i=i: e.tensor_tensor(
                T["xb"][i], T["xb"][i], P(7), OP.add), inc="vS")
            w.wait("scalar", "vS", w.n["vS"])
            w.emit("scalar", lambda e, i=i: e.activation(
                T["xb"][i], T["xb"][i], AF.Identity,
                scale=T["gamma_col"][:, i:i + 1],
                bias=T["beta_col"][:, i:i + 1]), inc="cS")
            xh_cs[i] = w.n["cS"]
            w.wait("gpsimd", "cS", xh_cs[i])
            w.dma("gpsimd", D["xh_own"][i * 128:(i + 1) * 128, c0:c0 + TC],
                  T["xb"][i], sHp)
        xh_vec_done[tc_i] = w.n["vS"]
        cbufs = T["cb"]
        w.wait("vector", "cS", xh_cs[ET - 1])
        w.wait("vector", sHp, w.n[sHp])
        for j in range(4):
            dst = cbufs[j]
            w.emit("vector", lambda e, j=j, dst=dst: e.tensor_scalar_mul(
                dst, T["xb"][j], T["selB"]))
            w.emit("vector", lambda e, j=j, dst=dst: e.scalar_tensor_tensor(
                dst, T["xb"][j + 4], T["selA"], dst, OP.mult, OP.add), inc="vS")
            w.wait("gpsimd", "vS", w.n["vS"])
            w.dma("gpsimd",
                  D["cc1_in"][tc_i * 512 + j * 128:tc_i * 512 + (j + 1) * 128, :],
                  dst, sHp)
        w.wait("gpsimd", sHp, w.n[sHp])
        w.emit("gpsimd", lambda e, tc_i=tc_i: e.collective_compute(
            "AllGather", OP.bypass,
            replica_groups=[[0, 1], [2, 3], [4, 5], [6, 7]],
            ins=[D["cc1_in"][tc_i * 512:(tc_i + 1) * 512, :]],
            outs=[D["cc1_out"][tc_i * 1024:(tc_i + 1) * 1024, :]]), inc="cc")

    if PHASES == "p1":
        finish_early()
        return

    w.wait("gpsimd", "cc", w.n["cc"])
    w.barrier()

    if PHASES == "cc1":
        finish_early()
        return

    for hl in range(4):
        r = hl * 128
        if hl > 0:
            w.wait("sync", "vS", w.n["vS"])
        w.dma("sync", T["xhh"][hl][:, 0:TOK], D["xh_own"][r:r + 128, :], "sH")
        w.dma("sync", T["xsc"][:, 0:TOK], D["xh_own"][512 + r:512 + r + 128, :],
              "sH")
        for tcj in range(NCH):
            base = tcj * 1024
            w.dma("sync", T["xhh"][hl][:, TOK + tcj * TC:TOK + (tcj + 1) * TC],
                  D["cc1_out"][base + 512 + r:base + 512 + r + 128, :], "sH")
            w.dma("sync", T["xsc"][:, TOK + tcj * TC:TOK + (tcj + 1) * TC],
                  D["cc1_out"][base + r:base + r + 128, :], "sH")
        w.wait("vector", "sH", w.n["sH"])
        w.emit("vector", lambda e, hl=hl: e.tensor_scalar_mul(
            T["xhh"][hl], T["xhh"][hl], T["selA"]))
        w.emit("vector", lambda e, hl=hl: e.scalar_tensor_tensor(
            T["xhh"][hl], T["xsc"], T["selB"], T["xhh"][hl], OP.mult, OP.add),
            inc="vS")
    w.emit("vector", lambda e: e.memset(T["Vt"][:, 128:KC * 129:129], 1.0), inc="vS")
    xhh_ready = w.n["vS"]
    if P2LVL == 1:
        finish_early()
        return

    tblld = {}
    uv_ready = {}
    strip_done = [0]

    def prep_head(hl):
        Qu, Qv, Kb, Vt = T["Qu"], T["Qv"], T["Kb"], T["Vt"]
        UVlo, UVup, ABt = T["UVlo"], T["UVup"], T["AB"]
        w.wait("sync", "vS", w.n["vS"])
        w.dma("sync", T["pca"], D["pca_lo_h"][hl], "sT")
        w.dma("sync", T["psa"], D["psa_lo_h"][hl], "sT")
        w.dma("sync", T["pe_hi"], D["pe_hi_h"][hl], "sT")
        tbl_ld = w.n["sT"]
        tblld[hl] = tbl_ld
        w.wait("tensor", "vS", xhh_ready)
        w.wait("tensor", "cS", w.n["cS"])
        bfree = {4: 0, 5: 0, 6: 0, 7: 0}

        def pp(t):
            return 4 + (t % 4)

        for t in range(QT):
            cs = slice(t * 512, t * 512 + 512)
            bk_ = pp(t)
            w.wait("tensor", "cS", bfree[bk_])
            w.emit("tensor", lambda e, cs=cs, hl=hl, bk_=bk_: e.matmul(
                P(bk_), T["wq"], T["xhh"][hl][:, cs], start=True, stop=True),
                inc="tS")
            w.wait("scalar", "tS", w.n["tS"])
            w.emit("scalar", lambda e, cs=cs, hl=hl, Qu=Qu, bk_=bk_: e.activation(
                Qu[:, cs], P(bk_), AF.Identity, bias=T["quv"][:, hl:hl + 1]))
            w.emit("scalar", lambda e, cs=cs, hl=hl, Qv=Qv, bk_=bk_: e.activation(
                Qv[:, cs], P(bk_), AF.Identity, bias=T["quv"][:, 4 + hl:5 + hl]),
                inc="cS")
            bfree[bk_] = w.n["cS"]
        for t in range(QT):
            cs = slice(t * 512, t * 512 + 512)
            bk_ = pp(t)
            w.wait("tensor", "cS", bfree[bk_])
            w.emit("tensor", lambda e, cs=cs, hl=hl, bk_=bk_: e.matmul(
                P(bk_), T["wk"], T["xhh"][hl][:, cs], start=True, stop=True),
                inc="tS")
            w.wait("scalar", "tS", w.n["tS"])
            w.emit("scalar", lambda e, cs=cs, Kb=Kb, bk_=bk_: e.activation(
                Kb[:, cs], P(bk_), AF.Identity, bias=T["bk"]), inc="cS")
            bfree[bk_] = w.n["cS"]
        for g in range(8):
            bk_ = pp(g)
            w.wait("tensor", "cS", bfree[bk_])
            for s4 in range(4):
                ck = g * 4 + s4
                w.emit("tensor", lambda e, ck=ck, s4=s4, hl=hl, bk_=bk_: e.matmul(
                    P(bk_)[:, s4 * 128:s4 * 128 + 128],
                    T["xhh"][hl][:, ck * 128:ck * 128 + 128], T["wv"],
                    start=True, stop=True), inc="tS")
            w.wait("scalar", "tS", w.n["tS"])
            out_ap = Vt.rearrange("p (t c) -> p t c", c=129)[:, g * 4:g * 4 + 4, 0:128]
            w.emit("scalar", lambda e, out_ap=out_ap, bk_=bk_: e.copy(
                out_ap, P(bk_)), inc="cS")
            bfree[bk_] = w.n["cS"]
        for t in range(QT):
            cs = slice(t * 512, t * 512 + 512)
            bk_ = pp(t)
            w.wait("tensor", "cS", bfree[bk_])
            w.emit("tensor", lambda e, cs=cs, Qv=Qv, bk_=bk_: e.matmul(
                P(bk_), T["wab"], Qv[:, cs], start=True, stop=True), inc="tS")
            w.wait("scalar", "tS", w.n["tS"])
            w.emit("scalar", lambda e, cs=cs, ABt=ABt, bk_=bk_: e.copy(
                ABt[:, cs], P(bk_)), inc="cS")
            bfree[bk_] = w.n["cS"]
        w.wait("vector", "cS", w.n["cS"])
        w.wait("vector", "tS", w.n["tS"])
        w.wait("vector", "sT", tbl_ld)

        ABs = T["ABswap"]
        w.wait("sync", "cS", w.n["cS"])
        w.wait("sync", "tS", w.n["tS"])
        w.dma("sync", ABs[0:64, :], ABt[64:128, :], "sSW")
        w.dma("sync", ABs[64:128, :], ABt[0:64, :], "sSW")
        ab_swap = w.n["sSW"]

        def build_uv(UV, pca, psa, shift):
            ncc = S - shift
            w.emit("vector", lambda e: e.tensor_tensor(
                T["T1"][:, 0:ncc], ABt[:, shift:S], pca[:, 0:ncc], OP.mult))
            w.emit("vector", lambda e: e.tensor_tensor(
                T["T2"][:, 0:ncc], ABs[:, shift:S], psa[:, 0:ncc], OP.mult),
                inc="vS")
            w.emit("vector", lambda e: e.tensor_tensor(
                UV[0:64, 0:ncc], T["T1"][0:64, 0:ncc], T["T2"][0:64, 0:ncc],
                OP.add))
            w.emit("vector", lambda e: e.tensor_sub(
                UV[64:128, 0:ncc], T["T1"][64:128, 0:ncc], T["T2"][64:128, 0:ncc]),
                inc="vS")
            if shift:
                w.emit("vector", lambda e: e.memset(UV[:, S - shift:S], 0.0),
                       inc="vS")

        w.wait("vector", "sSW", ab_swap)
        build_uv(UVlo, T["pca"], T["psa"], 0)
        w.wait("sync", "vS", w.n["vS"])
        w.dma("sync", T["pca"], D["pca_up_h"][hl], "sT")
        w.dma("sync", T["psa"], D["psa_up_h"][hl], "sT")
        w.wait("vector", "sT", w.n["sT"])
        build_uv(UVup, T["pca"], T["psa"], 1)
        uv_ready[hl] = w.n["vS"]

    def emit_bound():
        Qu, Kb, UVlo, UVup = T["Qu"], T["Kb"], T["UVlo"], T["UVup"]
        w.emit("vector", lambda e, Qu=Qu: e.reduce_max(
            T["mx"][:, 0:1], Qu, axis=AX.X, apply_absolute_value=True))
        w.emit("vector", lambda e, Kb=Kb: e.reduce_max(
            T["mx"][:, 1:2], Kb, axis=AX.X, apply_absolute_value=True))
        w.emit("vector", lambda e, UVlo=UVlo: e.reduce_max(
            T["mx"][:, 2:3], UVlo, axis=AX.X, apply_absolute_value=True))
        w.emit("vector", lambda e, UVup=UVup: e.reduce_max(
            T["mx"][:, 3:4], UVup, axis=AX.X, apply_absolute_value=True))
        w.emit("vector", lambda e: e.tensor_tensor(
            T["mx"][:, 4:5], T["mx"][:, 0:1], T["mx"][:, 1:2], OP.mult))
        w.emit("vector", lambda e: e.tensor_tensor(
            T["mx"][:, 5:6], T["mx"][:, 2:3], T["mx"][:, 3:4], OP.max))
        w.emit("vector", lambda e: e.tensor_copy(T["mxb"], T["mx"][:, 4:6]), inc="vS")
        w.wait("tensor", "vS", w.n["vS"])
        w.emit("tensor", lambda e: e.matmul(
            P(7)[0:1, 0:2], T["ones_col"], T["mxb"], start=True, stop=True), inc="tS")
        w.wait("vector", "tS", w.n["tS"])
        w.emit("vector", lambda e: e.tensor_copy(T["rs1"][0:1, 0:2], P(7)[0:1, 0:2]))
        w.emit("vector", lambda e: e.tensor_tensor(
            T["rs1"][0:1, 2:3], T["rs1"][0:1, 0:1], T["rs1"][0:1, 1:2], OP.add))
        w.emit("vector", lambda e: e.tensor_scalar(
            T["r_b"], T["rs1"][0:1, 2:3], -ISQ * 1.02, -0.02, OP.mult, OP.add),
            inc="vS")
        w.wait("tensor", "vS", w.n["vS"])
        w.emit("tensor", lambda e: e.matmul(
            P(7)[:, 8:9], T["ones_row"], T["r_b"], start=True, stop=True), inc="tS")
        w.wait("scalar", "tS", w.n["tS"])
        w.emit("scalar", lambda e: e.copy(T["negb"], P(7)[:, 8:9]), inc="cS")

    def head_scores(hl):
        Qu, Kb, Vt = T["Qu"], T["Kb"], T["Vt"]
        UVlo, UVup, pehi, negb = T["UVlo"], T["UVup"], T["pe_hi"], T["negb"]
        w.wait("tensor", "sT", tblld[hl])
        w.wait("tensor", "vS", uv_ready[hl])
        pv_done = {}
        exp_done = {}
        pend = [None]

        def evac_a(qt):
            w.wait("vector", "tS", w.n["tS"])
            for qs in range(4):
                w.emit("vector", lambda e, qs=qs: e.tensor_copy(
                    T["Ssum"][:, qs:qs + 1],
                    PB[4 + qs // 2][:, (qs % 2) * 129 + 128:(qs % 2) * 129 + 129]))
            w.emit("vector", lambda e: e.reciprocal(T["invS"], T["Ssum"]), inc="vS")
            w.wait("scalar", "vS", w.n["vS"])
            if pend[0] is not None:
                w.wait("scalar", "tS", pend[0][1])
            for qs in range(4):
                w.emit("scalar", lambda e, qs=qs: e.activation(
                    T["ctx_tm4"][qs],
                    PB[4 + qs // 2][:, (qs % 2) * 129:(qs % 2) * 129 + 128],
                    AF.Identity, scale=T["invS"][:, qs:qs + 1]), inc="cS")
            return w.n["cS"]

        def evac_b(hl, qt, evac_cs):
            w.wait("tensor", "cS", evac_cs)
            for qs in range(4):
                w.emit("tensor", lambda e, qs=qs: e.matmul(
                    P(7)[:, qs * 128:qs * 128 + 128],
                    T["ctx_tm4"][qs], T["ident"], start=True, stop=True), inc="tS")
            tp_done = w.n["tS"]
            w.wait("scalar", "tS", tp_done)
            for qs in range(4):
                w.emit("scalar", lambda e, hl=hl, qt=qt, qs=qs: e.activation(
                    T["ctxf"][hl][:, qt * 512 + qs * 128:qt * 512 + qs * 128 + 128],
                    P(7)[:, qs * 128:qs * 128 + 128],
                    AF.Identity, bias=T["bv"]), inc="cS")
            return tp_done

        def emit_pv(qt, c):
            ci = qt * KC + c
            w.wait("tensor", "cS", exp_done[ci])
            for qs in range(4):
                ctxap = PB[4 + qs // 2][:, (qs % 2) * 129:(qs % 2) * 129 + 129]
                w.emit("tensor", lambda e, c=c, qs=qs, ci=ci, Vt=Vt, ctxap=ctxap:
                       e.matmul(
                    ctxap,
                    T["Pt"][:, (ci % 4) * 512 + qs * 128:
                            (ci % 4) * 512 + qs * 128 + 128],
                    Vt[:, c * 129:c * 129 + 129],
                    start=(c == 0 and (qs % 2) == 0),
                    stop=(c == KC - 1 and (qs % 2) == 1)),
                    inc="tS" if (qs == 3 or c == KC - 1) else None)
            pv_done[ci] = w.n["tS"]

        for qt in range(QT):
            q0 = qt * 512
            for c in range(KC):
                ci = qt * KC + c
                sb = P(ci % 4)
                d0 = c * 128 - q0
                a = max(0, min(512, d0 - 1))
                b2 = max(0, min(512, d0 + 127))
                if ci >= 4:
                    w.wait("tensor", "cS", exp_done[ci - 4])
                mm_list = [lambda e, st, c=c, q0=q0, sb=sb, Qu=Qu, Kb=Kb: e.matmul(
                    sb, Kb[:, c * 128:c * 128 + 128], Qu[:, q0:q0 + 512],
                    start=True, stop=st)]
                if b2 < 512:
                    mm_list.append(
                        lambda e, st, c=c, q0=q0, sb=sb, b2=b2, UVlo=UVlo, pehi=pehi:
                        e.matmul(sb[:, b2:512], pehi[:, c * 128:c * 128 + 128],
                                 UVlo[:, q0 + b2:q0 + 512], start=False, stop=st))
                if a > 0:
                    mm_list.append(
                        lambda e, st, c=c, q0=q0, sb=sb, a=a, UVup=UVup, pehi=pehi:
                        e.matmul(sb[:, 0:a], pehi[:, c * 128:c * 128 + 128],
                                 UVup[:, q0:q0 + a], start=False, stop=st))
                for mi, mfn in enumerate(mm_list):
                    last = mi == len(mm_list) - 1
                    w.emit("tensor",
                           lambda e, mfn=mfn, last=last: mfn(e, last),
                           inc="tS" if last else None)
                if c == 2 and pend[0] is not None:
                    pqt, _, pcs = pend[0]
                    pend[0] = (pqt, evac_b(hl, pqt, pcs), pcs)
                mm_here = w.n["tS"]
                vec_here = w.n["vS"]
                if a < b2 and not NOSTRIP:
                    wdt = b2 - a
                    moff = a - (d0 - 1)
                    w.wait("tensor", "vS", strip_done[0])
                    w.emit("tensor",
                           lambda e, c=c, q0=q0, a=a, wdt=wdt, UVlo=UVlo, pehi=pehi:
                           e.matmul(P(6)[:, 0:wdt], pehi[:, c * 128:c * 128 + 128],
                                    UVlo[:, q0 + a:q0 + a + wdt],
                                    start=True, stop=True), inc="tS")
                    w.emit("tensor",
                           lambda e, c=c, q0=q0, a=a, wdt=wdt, UVup=UVup, pehi=pehi:
                           e.matmul(P(6)[:, 128:128 + wdt],
                                    pehi[:, c * 128:c * 128 + 128],
                                    UVup[:, q0 + a:q0 + a + wdt],
                                    start=True, stop=True), inc="tS")
                    mm_here = w.n["tS"]
                    w.wait("vector", "tS", mm_here)
                    w.emit("vector", lambda e, wdt=wdt, moff=moff: e.tensor_tensor(
                        T["strip"][:, 0:wdt], P(6)[:, 0:wdt],
                        T["m_lo"][:, moff:moff + wdt], OP.mult))
                    w.emit("vector", lambda e, wdt=wdt, moff=moff: e.tensor_tensor(
                        T["strip"][:, 128:128 + wdt], P(6)[:, 128:128 + wdt],
                        T["m_up"][:, moff:moff + wdt], OP.mult))
                    w.emit("vector", lambda e, wdt=wdt: e.tensor_tensor(
                        T["strip"][:, 0:wdt], T["strip"][:, 0:wdt],
                        T["strip"][:, 128:128 + wdt], OP.add))
                    w.emit("vector", lambda e, sb=sb, a=a, wdt=wdt: e.tensor_tensor(
                        sb[:, a:a + wdt], sb[:, a:a + wdt], T["strip"][:, 0:wdt],
                        OP.add), inc="vS")
                    vec_here = w.n["vS"]
                    strip_done[0] = w.n["vS"]
                if qt == 0 and c == 0:
                    emit_bound()
                if c >= 2 and HSLVL >= 2:
                    emit_pv(qt, c - 2)
                w.wait("scalar", "tS", mm_here)
                w.wait("scalar", "vS", vec_here)
                if ci >= 4 and (ci - 4) in pv_done:
                    w.wait("scalar", "tS", pv_done[ci - 4])
                if HSLVL >= 1:
                    w.emit("scalar", lambda e, ci=ci, sb=sb, negb=negb: e.activation(
                        T["Pt"][:, (ci % 4) * 512:(ci % 4) * 512 + 512], sb,
                        AF.Exp, bias=negb, scale=T["c_isq"]), inc="cS")
                else:
                    w.emit("scalar", lambda e: e.nop(), inc="cS")
                exp_done[ci] = w.n["cS"]
            if HSLVL >= 2:
                emit_pv(qt, KC - 2)
                emit_pv(qt, KC - 1)
            if HSLVL < 3:
                continue
            ecs = evac_a(qt)
            if qt < QT - 1:
                pend[0] = (qt, None, ecs)
            else:
                evac_b(hl, qt, ecs)
                pend[0] = None
                w.wait("tensor", "cS", w.n["cS"])
        w.wait("vector", "cS", w.n["cS"])
        w.wait("vector", "gS", w.n["gS"])
        xs = T["xsc"]
        w.emit("vector", lambda e, hl=hl, xs=xs: e.tensor_scalar_mul(
            xs[:, 0:TOK], T["ctxf"][hl][:, 0:TOK], T["selB"]))
        w.emit("vector", lambda e, hl=hl, xs=xs: e.scalar_tensor_tensor(
            xs[:, 0:TOK], T["ctxf"][hl][:, TOK:S], T["selA"], xs[:, 0:TOK],
            OP.mult, OP.add))
        w.emit("vector", lambda e, hl=hl, xs=xs: e.tensor_scalar_mul(
            xs[:, TOK:S], T["ctxf"][hl][:, 0:TOK], T["selA"]))
        w.emit("vector", lambda e, hl=hl, xs=xs: e.scalar_tensor_tensor(
            xs[:, TOK:S], T["ctxf"][hl][:, TOK:S], T["selB"], xs[:, TOK:S],
            OP.mult, OP.add), inc="vS")
        w.wait("gpsimd", "vS", w.n["vS"])
        w.dma("gpsimd", D["cc2_in"][hl * 128:(hl + 1) * 128, :], xs[:, 0:TOK], "gS")
        w.dma("gpsimd", D["ctx_own"][hl * 128:(hl + 1) * 128, :], xs[:, TOK:S], "gS")
        w.wait("gpsimd", "gS", w.n["gS"])
        w.emit("gpsimd", lambda e, hl=hl: e.collective_compute(
            "AllGather", OP.bypass,
            replica_groups=[[0, 1], [2, 3], [4, 5], [6, 7]],
            ins=[D["cc2_in"][hl * 128:(hl + 1) * 128, :]],
            outs=[D["cc2_out"][hl * 256:(hl + 1) * 256, :]]), inc="cc")

    for hl in range(4):
        prep_head(hl)
        if P2LVL == 2 and hl == 0:
            finish_early()
            return
        head_scores(hl)
        if P2LVL == 3 and hl == 0:
            finish_early()
            return

    if PHASES == "p2":
        finish_early()
        return

    w.wait("gpsimd", "cc", w.n["cc"])
    w.barrier()

    if PHASES == "cc2":
        finish_early()
        return

    for i in range(ET):
        w.dma("sync", T["wo_sb"][i], D["wo_bf"][i * 128:(i + 1) * 128, :], "sS")
    wts_ld = w.n["sS"]

    out_store = {}
    wo_mm_done = {}
    for tc_i in range(NCH):
        c0 = tc_i * TC
        T["xb"] = T["xb0"] if (tc_i % 2 == 0) else T["xb1"]
        ctx_sel_last = 0
        for i in range(ET):
            if tc_i > 0 and i == 0:
                w.wait("sync", "tS", wo_mm_done[tc_i - 1])
            if i > 0 or tc_i > 0:
                w.wait("sync", "vS", w.n["vS"])
            w.dma("sync", T["ctxc"][i],
                  D["ctx_own"][(i % 4) * 128:(i % 4) * 128 + 128, c0:c0 + TC], "sC")
            rb = 256 * (i % 4) + 128 * (i // 4)
            w.dma("sync", T["csc"],
                  D["cc2_out"][rb:rb + 128, c0:c0 + TC], "sC")
            w.wait("vector", "sC", w.n["sC"])
            w.emit("vector", lambda e, i=i: e.tensor_scalar_mul(
                T["ctxc"][i], T["ctxc"][i], T["selT"][:, i:i + 1]))
            w.emit("vector", lambda e, i=i: e.scalar_tensor_tensor(
                T["ctxc"][i], T["csc"], T["selT2"][:, i:i + 1], T["ctxc"][i],
                OP.mult, OP.add), inc="vS")
            ctx_sel_last = w.n["vS"]
        if tc_i == 0:
            for i in range(ET):
                w.dma("sync", T["wg_t"][i],
                      D["wg_bf"][i * 128:(i + 1) * 128, :], "sX2")
            wg_ld = w.n["sX2"]
        x1_ld_t = {}

        def issue_x1(i, tc_i=tc_i, c0=c0):
            sr = ("sRA", "sRB", "sXA", "sXB")[i % 4]
            if tc_i > 0:
                sgo, vo = out_store[(tc_i - 1, i)]
                w.wait("sync", sgo, vo)
            w.dma("sync", T["x2"][i],
                  D["x1"][i * 128:(i + 1) * 128, c0:c0 + TC], sr)
            x1_ld_t[i] = (sr, w.n[sr])

        for i in range(4):
            issue_x1(i)
        if tc_i == 0:
            w.wait("tensor", "sS", wts_ld)
        w.wait("tensor", "vS", ctx_sel_last)
        wo_sem = {}
        for eo in range(ET):
            wb = P(0 + (eo % 2))
            if eo >= 2:
                w.wait("tensor", "vS", wo_sem[eo - 2])
            else:
                w.wait("tensor", "vS", w.n["vS"])
            for ei in range(ET):
                w.emit("tensor", lambda e, eo=eo, ei=ei, wb=wb: e.matmul(
                    wb, T["wo_sb"][ei][:, eo * 128:eo * 128 + 128], T["ctxc"][ei],
                    start=(ei == 0), stop=(ei == ET - 1)),
                    inc="tS" if ei == ET - 1 else None)
            w.wait("vector", "tS", w.n["tS"])
            sr_, v_ = x1_ld_t[eo]
            w.wait("vector", sr_, v_)
            w.emit("vector", lambda e, eo=eo, wb=wb: e.scalar_tensor_tensor(
                T["x2"][eo], wb, T["bo_col"][:, eo:eo + 1], T["x2"][eo],
                OP.add, OP.add), inc="vS")
            wo_sem[eo] = w.n["vS"]
            if eo + 4 < ET:
                w.wait("sync", "vS", wo_sem[eo])
                issue_x1(eo + 4)
            if eo == ET - 1:
                wo_mm_done[tc_i] = w.n["tS"]
        w.wait("scalar", "vS", w.n["vS"])
        w.wait("scalar", "tS", w.n["tS"])
        st3_done = {}
        st3_cast = {}

        def emit_st3(j):
            if j == 0:
                w.wait("tensor", "vS", w.n["vS"])
            w.wait("tensor", "cS", st3_cast[j])
            w.emit("tensor", lambda e, j=j: e.matmul(
                T["P6s"], T["ones_col"], T["xb"][j],
                start=(j == 0), stop=(j == ET - 1)), inc="tS")
            w.emit("tensor", lambda e, j=j: e.matmul(
                T["P7s"], T["ones_col"], T["sqs"][j],
                start=(j == 0), stop=(j == ET - 1)), inc="tS")
            st3_done[j] = w.n["tS"]

        for i in range(ET):
            if i >= 2:
                emit_st3(i - 2)
                w.wait("scalar", "tS", st3_done[i - 2])
            w.emit("scalar", lambda e, i=i: e.copy(T["xb"][i], T["x2"][i]), inc="cS")
            w.emit("scalar", lambda e, i=i: e.activation(
                T["sqs"][i], T["x2"][i], AF.Square), inc="cS")
            st3_cast[i] = w.n["cS"]
        emit_st3(ET - 2)
        emit_st3(ET - 1)
        ln_rows(fused_double=False)
        ln_apply()

        def store_out(et, tc_i=tc_i, c0=c0):
            sgo = f"sO{et}"
            w.wait("gpsimd", "vS", w.n["vS"])
            w.dma("gpsimd", D["out"][et * 128:(et + 1) * 128, c0:c0 + TC],
                  T["x2"][et], sgo, inc_n=64)
            out_store[(tc_i, et)] = (sgo, w.n[sgo])

        if tc_i == 0:
            w.wait("tensor", "sX2", wg_ld)
        last_silu[0] = ffn_body(T["x2"], store_out,
                                entry_cs=last_silu[0], entry_gs=False)

    for eng in ENGINES:
        w.wait(eng, "gS", w.n["gS"])
        for j in range(ET):
            w.wait(eng, f"sO{j}", w.n[f"sO{j}"])


def build_nc():
    nc = bass.Bass("TRN2", target_bir_lowering=False)

    D = {}
    D["x"] = nc.dram_tensor("x", [E, TOK], F32, kind="ExternalInput").ap()
    for nm, shp in (("wq", [128, 128]), ("wk", [128, 128]), ("wv", [128, 128]),
                    ("wab", [128, 128]), ("wo", [E, E]), ("bq", [128, 1]),
                    ("bk", [128, 1]), ("bv", [128, 1]), ("ub", [128, 4]),
                    ("vb", [128, 4]), ("gamma", [1, E]), ("beta_col", [128, 8]),
                    ("gamma_col", [128, 8]),
                    ("bo_col", [128, 8]), ("selA", [128, 1]), ("selB", [128, 1]),
                    ("immc", [128, 2]), ("selT", [128, 8]),
                    ("m_lo", [128, 128]), ("m_up", [128, 128])):
        D[nm] = nc.dram_tensor(nm, shp, F32, kind="ExternalInput").ap()
    for nm, shp in (("wg_i32", [E, HID]), ("wu_i32", [E, HID]),
                    ("wd_i32", [HID, E])):
        D[nm] = nc.dram_tensor(nm, shp, I32, kind="ExternalInput").ap()
    for nm, shp in (("ident", [128, 128]), ("ones_col", [128, 1]),
                    ("ones_row", [1, 128])):
        D[nm] = nc.dram_tensor(nm, shp, BF16, kind="ExternalInput").ap()
    for nm in ("pca_lo", "psa_lo", "pca_up", "psa_up", "pe_hi"):
        t_ = nc.dram_tensor(nm, [512, S], BF16, kind="ExternalInput").ap()
        D[nm + "_h"] = [t_[hl * 128:(hl + 1) * 128, :] for hl in range(4)]

    D["out"] = nc.dram_tensor("out", [E, TOK], F32, kind="ExternalOutput").ap()

    D["x1"] = nc.dram_tensor("x1_scr", [E, TOK], F32).ap()
    D["wg_bf"] = nc.dram_tensor("wg_bf", [E, HID], BF16).ap()
    D["wu_sw"] = nc.dram_tensor("wu_sw", [HT * 128, 1024], BF16).ap()
    D["wd_cache"] = nc.dram_tensor("wd_cache", [ET * 128, HID], BF16).ap()
    D["wo_bf"] = nc.dram_tensor("wo_bf", [E, E], BF16).ap()
    D["xh_own"] = nc.dram_tensor("xh_own", [E, TOK], BF16).ap()
    D["ctx_own"] = nc.dram_tensor("ctx_own", [512, TOK], BF16).ap()
    D["cc1_in"] = nc.dram_tensor("cc1_in", [NCH * 512, TC], BF16)
    D["cc1_out"] = nc.dram_tensor("cc1_out", [NCH * 1024, TC], BF16)
    D["cc2_in"] = nc.dram_tensor("cc2_in", [4 * 128, TOK], BF16)
    D["cc2_out"] = nc.dram_tensor("cc2_out", [4 * 256, TOK], BF16)

    from contextlib import ExitStack
    with ExitStack() as ctx:
        BF_CAP = 182 * 1024
        F_CAP = 23 * 1024
        ab = ctx.enter_context(nc.sbuf_tensor("arena_bf", [128, BF_CAP // 2], BF16))
        af = ctx.enter_context(nc.sbuf_tensor("arena_f32", [128, F_CAP // 4], F32))

        T = {}
        fo = [0]
        bo = [0]

        def takef(ncols):
            off = fo[0]
            assert (off * 4) % 32 == 0
            fo[0] += ncols
            fo[0] = (fo[0] + 7) & ~7
            assert fo[0] * 4 <= F_CAP, f"f32 arena {fo[0]*4}"
            return af[:, off:off + ncols], off

        def takeb(ncols):
            off = bo[0]
            bo[0] += ncols
            bo[0] = (bo[0] + 15) & ~15
            assert bo[0] * 2 <= BF_CAP, f"bf arena {bo[0]*2}"
            return ab[:, off:off + ncols], off

        T["m_lo"], _ = takef(128)
        T["m_up"], _ = takef(128)
        smalls, _ = takef(64)
        T["bq"] = smalls[:, 0:1]
        T["bk"] = smalls[:, 1:2]
        T["bv"] = smalls[:, 2:3]
        T["selA"] = smalls[:, 3:4]
        T["selB"] = smalls[:, 4:5]
        T["negb"] = smalls[:, 5:6]
        T["ub"] = smalls[:, 8:12]
        T["vb"] = smalls[:, 12:16]
        T["quv"] = smalls[:, 16:24]
        T["beta_col"] = smalls[:, 24:32]
        T["bo_col"] = smalls[:, 32:40]
        T["mx"] = smalls[:, 40:48]
        T["Ssum"] = smalls[:, 48:52]
        T["invS"] = smalls[:, 52:56]
        T["immc"] = smalls[:, 56:58]
        selt, _ = takef(8)
        T["selT"] = selt
        selt2, _ = takef(8)
        T["selT2"] = selt2
        gcol, _ = takef(8)
        T["gamma_col"] = gcol
        rsblk, _ = takef(512)
        T["rs1"] = rsblk[0:1, :]
        rsblk2, _ = takef(512)
        T["rs2"] = rsblk2[0:1, :]
        T["c_qdeq"] = smalls[:, 56:57]
        T["c_isq"] = smalls[:, 57:58]

        T["ident"], _ = takeb(128)
        wqkv, _ = takeb(4 * 128)
        T["wq"] = wqkv[:, 0:128]
        T["wk"] = wqkv[:, 128:256]
        T["wv"] = wqkv[:, 256:384]
        T["wab"] = wqkv[:, 384:512]
        oc, _ = takeb(16)
        T["ones_col"] = oc[:, 0:1]
        orow, _ = takeb(128)
        T["ones_row"] = orow[0:1, :]
        grow, _ = takeb(1024)
        T["gamma_row"] = grow[0:1, :]
        ra, _ = takeb(TC)
        T["rowA"] = ra[0:1, :]
        rb_, _ = takeb(TC)
        T["rowB"] = rb_[0:1, :]
        rbb, _ = takeb(16)
        T["r_b"] = rbb[0:1, 0:1]
        mxbb, _ = takeb(16)
        T["mxb"] = mxbb[:, 0:2]

        f_mark = fo[0]
        T["xf0"], xf0_off = takef(TC)
        T["xf1"], _ = takef(TC)
        T["xf2"], _ = takef(TC)
        T["xf3"], _ = takef(TC)
        T["xf01_row"] = af[0:1, xf0_off:xf0_off + 1024]
        T["xf01_blk"] = af[:, xf0_off:xf0_off + 1024]
        p1_f = fo[0]
        fo[0] = f_mark
        T["strip"], _ = takef(256)
        p2_f = fo[0]
        fo[0] = f_mark
        T["x2"] = [takef(TC)[0] for _ in range(ET)]
        p34_f = fo[0]
        fo[0] = max(p1_f, p2_f, p34_f)
        assert fo[0] * 4 <= F_CAP, f"f32 total {fo[0]*4}"

        b_mark = bo[0]
        T["wg_t"] = [takeb(HID)[0] for _ in range(ET)]
        T["wubuf"] = [takeb(1024)[0] for _ in range(2)]
        T["wdbuf"] = [takeb(HID)[0] for _ in range(2)]
        T["wo_sb"] = [takeb(1024)[0] for _ in range(ET)]
        T["xb0"] = [takeb(TC)[0] for _ in range(ET)]
        T["xb1"] = [takeb(TC)[0] for _ in range(ET)]
        h_t = []
        h_offs = []
        for _ in range(HT):
            t_, o_ = takeb(TC)
            h_t.append(t_)
            h_offs.append(o_)
        T["h_t"] = h_t
        T["silu"] = [takeb(TC)[0] for _ in range(2)]
        T["sqs"] = [takeb(TC)[0] for _ in range(8)]
        T["sqs2"] = [takeb(TC)[0] for _ in range(2)]
        T["cb"] = [takeb(TC)[0] for _ in range(4)]
        T["ctxc"] = [takeb(TC)[0] for _ in range(ET)]
        T["csc"], _ = takeb(TC)
        p14_b = bo[0]
        h0 = h_offs[0]
        T["istage"] = [
            ab[:, h0:h0 + 3072].bitcast(I32),
            ab[:, h0 + 3072:h0 + 6144].bitcast(I32),
        ]
        T["stagebf"] = [ab[:, h0 + 6144:h0 + 7168], ab[:, h0 + 7168:h0 + 8192]]
        T["wuconv"] = ab[:, h0 + 8192:h0 + 11264]
        bo[0] = b_mark
        T["xhh"] = [takeb(S)[0] for _ in range(4)]

        T["xsc"], _ = takeb(S)
        T["Qu"], _ = takeb(S)
        T["Qv"], _ = takeb(S)
        T["ABswap"] = T["Qv"]
        T["Kb"], _ = takeb(S)
        T["Vt"], _ = takeb(KC * 129)
        T["AB"], _ = takeb(S)
        T["UVlo"], _ = takeb(S)
        T["UVup"], _ = takeb(S)
        T["T1"], _ = takeb(S)
        T["T2"], _ = takeb(S)
        T["pca"], _ = takeb(S)
        T["psa"], _ = takeb(S)
        T["pe_hi"], _ = takeb(S)
        T["Pt"], _ = takeb(4 * 512)
        ctxtm_, _ = takeb(4 * 128)
        T["ctx_tm4"] = [ctxtm_[:, j * 128:(j + 1) * 128] for j in range(4)]
        T["ctxf"] = [takeb(S)[0] for _ in range(4)]
        p2_b = bo[0]
        bo[0] = max(p14_b, p2_b)
        assert bo[0] * 2 <= BF_CAP, f"bf16 total {bo[0]*2}"

        psA = ctx.enter_context(nc.psum_tensor("psA", [128, 1024], F32))
        psB = ctx.enter_context(nc.psum_tensor("psB", [128, 1024], F32))
        pb4 = ctx.enter_context(nc.psum_tensor("pb4", [128, 512], F32))
        pb5 = ctx.enter_context(nc.psum_tensor("pb5", [128, 512], F32))
        pb6 = ctx.enter_context(nc.psum_tensor("pb6", [128, 512], F32))
        pb7 = ctx.enter_context(nc.psum_tensor("pb7", [128, 512], F32))
        PB = [psA[:, 0:512], psA[:, 512:1024], psB[:, 0:512], psB[:, 512:1024],
              pb4[:, :], pb5[:, :], pb6[:, :], pb7[:, :]]
        T["P6s"] = pb6[0:1, 0:TC]
        T["P7s"] = pb7[0:1, 0:TC]

        sems = {}
        for s_ in ("sS", "gS", "tS", "vS", "cS", "bar", "sXA", "sXB", "sRA",
                   "sRB", "sWA", "sWB", "sT", "sH", "sC", "sX2", "sGA", "sGB",
                   "cc", "sSW", "sHA", "sHB", "sUA", "sUB",
                   "sO0", "sO1", "sO2", "sO3", "sO4", "sO5", "sO6", "sO7"):
            sems[s_] = ctx.enter_context(nc.semaphore(s_))

        block = ctx.enter_context(nc.Block())

        @block.sync
        def _(sync):
            _program(W("sync", sync, sems), T, D, PB)

        @block.tensor
        def _(tensor):
            _program(W("tensor", tensor, sems), T, D, PB)

        @block.vector
        def _(vector):
            _program(W("vector", vector, sems), T, D, PB)

        @block.scalar
        def _(scalar):
            _program(W("scalar", scalar, sems), T, D, PB)

        @block.gpsimd
        def _(gpsimd):
            _program(W("gpsimd", gpsimd, sems), T, D, PB)

    return nc


_NC = None


def kernel(L0_attn_inputs, L0_attn_ln_gamma, L0_attn_ln_beta, L0_attn_Wq,
           L0_attn_bq, L0_attn_Wk, L0_attn_bk, L0_attn_Wv, L0_attn_bv,
           L0_attn_Wp, L0_attn_u_bias, L0_attn_v_bias, L0_attn_Wo, L0_attn_bo,
           L0_ffn_w_gate, L0_ffn_w_up, L0_ffn_w_down):
    global _NC
    if _NC is None:
        _NC = build_nc()
    nc = _NC
    C = _consts()

    x = np.asarray(L0_attn_inputs, dtype=np.float32)
    Wp = np.asarray(L0_attn_Wp, dtype=np.float32)
    wab = np.ascontiguousarray(np.concatenate([Wp[0::2, :], Wp[1::2, :]], axis=0).T)
    shared = dict(
        wq=np.ascontiguousarray(L0_attn_Wq, dtype=np.float32),
        wk=np.ascontiguousarray(L0_attn_Wk, dtype=np.float32),
        wv=np.ascontiguousarray(L0_attn_Wv, dtype=np.float32),
        wab=wab,
        wo=np.ascontiguousarray(L0_attn_Wo, dtype=np.float32),
        bq=np.ascontiguousarray(np.asarray(L0_attn_bq, np.float32)[:, None]),
        bk=np.ascontiguousarray(np.asarray(L0_attn_bk, np.float32)[:, None]),
        bv=np.ascontiguousarray(np.asarray(L0_attn_bv, np.float32)[:, None]),
        gamma=np.ascontiguousarray(np.asarray(L0_attn_ln_gamma, np.float32)[None, :]),
        beta_col=np.ascontiguousarray(
            np.asarray(L0_attn_ln_beta, np.float32).reshape(8, 128).T),
        gamma_col=np.ascontiguousarray(
            np.asarray(L0_attn_ln_gamma, np.float32).reshape(8, 128).T),
        bo_col=np.ascontiguousarray(
            np.asarray(L0_attn_bo, np.float32).reshape(8, 128).T),
        wg_i32=np.ascontiguousarray(np.asarray(L0_ffn_w_gate, np.int32).T),
        wu_i32=np.ascontiguousarray(np.asarray(L0_ffn_w_up, np.int32).T),
        wd_i32=np.ascontiguousarray(np.asarray(L0_ffn_w_down, np.int32).T),
        ident=C["ident"], m_lo=C["m_lo"], m_up=C["m_up"],
        ones_col=C["ones_col"], ones_row=C["ones_row"],
        immc=np.ascontiguousarray(
            np.broadcast_to(np.array([QDEQ, ISQ], np.float32), (128, 2))),
    )
    ub_t = np.asarray(L0_attn_u_bias, np.float32).T
    vb_t = np.asarray(L0_attn_v_bias, np.float32).T

    in_maps = []
    for c in range(8):
        b = c // 2
        half = c % 2
        t0 = half * TOK
        hsl = slice(half * 512, half * 512 + 512)
        m = dict(shared)
        m["x"] = np.ascontiguousarray(x[b, t0:t0 + TOK, :].T)
        m["ub"] = np.ascontiguousarray(ub_t[:, 4 * half:4 * half + 4])
        m["vb"] = np.ascontiguousarray(vb_t[:, 4 * half:4 * half + 4])
        m["selA"] = np.full((128, 1), 1.0 if half == 0 else 0.0, np.float32)
        m["selB"] = np.full((128, 1), 0.0 if half == 0 else 1.0, np.float32)
        st_ = np.zeros((128, 8), np.float32)
        for i_ in range(8):
            st_[:, i_] = 1.0 if ((i_ < 4) == (half == 0)) else 0.0
        m["selT"] = st_
        for nm in ("pca_lo", "psa_lo", "pca_up", "psa_up", "pe_hi"):
            m[nm] = np.ascontiguousarray(C[nm][hsl])
        in_maps.append(m)

    res = run_bass_kernel_spmd(nc, in_maps, core_ids=list(range(8)))

    out = np.empty((B, S, E), dtype=np.float32)
    for c in range(8):
        b = c // 2
        t0 = (c % 2) * TOK
        out[b, t0:t0 + TOK, :] = res.results[c]["out"].T
    return out
